# version 11
# baseline (speedup 1.0000x reference)
import contextlib
import numpy as np
import ml_dtypes
import concourse.bass as bass
import concourse.mybir as mybir
from concourse.bass_utils import run_bass_kernel_spmd

F32 = mybir.dt.float32
BF16 = mybir.dt.bfloat16
U32 = mybir.dt.uint32
I32 = mybir.dt.int32
AF = mybir.ActivationFunctionType
ALU = mybir.AluOpType
AX = mybir.AxisListType
BF = ml_dtypes.bfloat16
IOA = bass.IndirectOffsetOnAxis

ENGS = ("tensor", "vector", "scalar", "sync", "gpsimd")


class Cfg:
    def __init__(self, D=4096, DI=8192, H=128, G=8, NST=128, CK=4, C=256,
                 T=2048, INTER=11008, batch=4, n_cores=8,
                 eps_ln=1e-5, eps_rms=1e-5):
        self.D, self.DI, self.H, self.G, self.NST, self.CK, self.C, self.T = \
            D, DI, H, G, NST, CK, C, T
        self.INTER, self.batch, self.n_cores = INTER, batch, n_cores
        self.eps_ln, self.eps_rms = eps_ln, eps_rms
        self.P = DI // H
        self.GN = G * NST
        self.conv_dim = DI + 2 * self.GN
        self.d_in = 2 * DI + 2 * self.GN + H
        self.KD = D // 128
        self.KDI = DI // 128
        self.KF = INTER // 128
        self.NB = self.GN // 128
        self.NXBC = self.conv_dim // 128
        self.NCH = T // C
        self.M_IN = (self.d_in + 127) // 128
        self.JTOT = 2 * INTER
        self.G_GU = D // 128
        self.G_D = INTER // 128
        self.TT = T // 128
        assert D % 128 == 0 and DI % 128 == 0 and self.GN % 128 == 0
        assert INTER % 128 == 0 and T % C == 0 and C == 256
        assert H <= 128 and self.P == 64 and NST == 128 and self.H % self.G == 0


CFG_REAL = Cfg()


class Sem:
    __slots__ = ("h", "count", "name")


class B:

    def __init__(self, nc, es):
        self.nc, self.es = nc, es
        self.sems = {}
        self.seen = {e: {} for e in ENGS}
        self.thunks = {e: [] for e in ENGS}

    def sem(self, name):
        if name not in self.sems:
            s = Sem()
            s.h = self.es.enter_context(self.nc.semaphore(name))
            s.count = 0
            s.name = name
            self.sems[name] = s
        return self.sems[name]

    def wait(self, eng, tok):
        if tok is None:
            return
        sem, v = tok
        if v <= 0 or self.seen[eng].get(sem.name, 0) >= v:
            return
        self.seen[eng][sem.name] = v
        self.thunks[eng].append(lambda e, s=sem.h, vv=v: e.wait_ge(s, vv))

    def emit(self, eng, fn, waits=(), inc=None, n=1):
        for t in waits:
            if isinstance(t, list):
                for tt in t:
                    self.wait(eng, tt)
            else:
                self.wait(eng, t)
        if inc is None:
            self.thunks[eng].append(fn)
            return None
        inc.count += n
        tok = (inc, inc.count)
        self.thunks[eng].append(lambda e, f=fn, s=inc.h, nn=n: f(e).then_inc(s, nn))
        return tok

    def barrier(self):
        toks = [(s, s.count) for s in self.sems.values() if s.count > 0]
        for e in ENGS:
            for t in toks:
                self.wait(e, t)

    def flush(self):
        if not any(self.thunks[e] for e in ENGS):
            return
        with self.nc.Block() as block:
            for ename in ENGS:
                lst = self.thunks[ename]
                if lst:
                    getattr(block, ename)(lambda e, L=lst: [f(e) for f in L])
        self.thunks = {e: [] for e in ENGS}


def _c(x):
    return np.ascontiguousarray(x)


def make_consts(cfg):
    c = {}
    c["ident_f"] = np.eye(128, dtype=np.float32)
    c["ident_b"] = np.eye(128, dtype=BF)
    i64 = np.zeros((128, 64), dtype=BF)
    i64[:64, :] = np.eye(64, dtype=BF)
    i64[64:, :] = np.eye(64, dtype=BF)
    c["i64_2"] = i64
    c["ones1_b"] = np.ones((1, 128), dtype=BF)
    c["ones1_f"] = np.ones((1, 128), dtype=np.float32)
    c["ones128_f"] = np.ones((128, 1), dtype=np.float32)
    c["onesNCH_b"] = np.ones((cfg.NCH, 128), dtype=BF)
    m = np.zeros((2, 128, cfg.C), dtype=BF)
    for sh in range(2):
        s = np.arange(128)[:, None] + 128 * sh
        l = np.arange(cfg.C)[None, :]
        m[sh] = (l >= s).astype(BF)
    c["masks_b"] = m
    pmod = np.arange(128, dtype=np.uint64) % 8
    c["m15"] = (np.uint32(15) << (4 * pmod).astype(np.uint32)).astype(np.uint32).reshape(128, 1)
    c["minv"] = (2.0 ** -(4.0 * pmod)).astype(np.float32).reshape(128, 1)
    c["iota_gu"] = np.broadcast_to(np.arange(cfg.G_GU, dtype=np.float32),
                                   (128, cfg.G_GU)).copy()
    c["iota_d"] = np.broadcast_to(np.arange(cfg.G_D, dtype=np.float32),
                                  (128, cfg.G_D)).copy()
    vals = _fconst_vals(cfg)
    c["fconst"] = np.broadcast_to(np.asarray(vals, np.float32), (128, len(vals))).copy()
    return c


def _fconst_vals(cfg):
    return [0.0, 1.0, -1.0, float(cfg.eps_ln), float(cfg.eps_rms),
            1.0 / cfg.D, 1.0 / cfg.DI]


def make_in_maps(cfg, inp):
    x = np.asarray(inp["L0_attn_hidden_states"])
    w_in = np.asarray(inp["L0_attn_in_proj_w"])
    conv_w = np.asarray(inp["L0_attn_conv_w"])
    conv_b = np.asarray(inp["L0_attn_conv_b"])
    dt_bias = np.asarray(inp["L0_attn_dt_bias"])
    a_log = np.asarray(inp["L0_attn_A_log"])
    dvec = np.asarray(inp["L0_attn_D"])
    norm_w = np.asarray(inp["L0_attn_norm_w"])
    w_out = np.asarray(inp["L0_attn_out_proj_w"])
    qw_gu = np.asarray(inp["L0_ffn_qweight_gu"])
    qz_gu = np.asarray(inp["L0_ffn_qzeros_gu"])
    sc_gu = np.asarray(inp["L0_ffn_scales_gu"])
    gi_gu = np.asarray(inp["L0_ffn_g_idx_gu"])
    qw_d = np.asarray(inp["L0_ffn_qweight_down"])
    qz_d = np.asarray(inp["L0_ffn_qzeros_down"])
    sc_d = np.asarray(inp["L0_ffn_scales_down"])
    gi_d = np.asarray(inp["L0_ffn_g_idx_down"])

    M, KD = cfg.M_IN, cfg.KD
    JSLAB = min(1280, cfg.JTOT)
    nsl = (cfg.JTOT + JSLAB - 1) // JSLAB
    sc_sl = np.zeros((nsl, sc_gu.shape[0], JSLAB), np.float32)
    for si in range(nsl):
        o = si * JSLAB
        w = min(JSLAB, cfg.JTOT - o)
        sc_sl[si, :, :w] = sc_gu[:, o:o + w]
    sc_sl = sc_sl.reshape(-1, JSLAB)
    w_in_p = np.zeros((M * 128, cfg.D), dtype=w_in.dtype)
    w_in_p[:cfg.d_in] = w_in
    h_in = _c(w_in_p.reshape(M, 128, KD, 128).transpose(0, 3, 2, 1))
    shared = dict(
        h_in=h_in,
        woutT=_c(w_out.T),
        qw_gu=_c(np.repeat(qw_gu.reshape(KD, 16, cfg.JTOT), 8, axis=1)),
        qz_gu=_c(qz_gu), sc_gu=_c(sc_gu), sc_gu_sl=_c(sc_sl),
        gidx_gu=_c(gi_gu.reshape(KD, 128).T.astype(np.int32)),
        qw_d=_c(np.repeat(qw_d.reshape(cfg.KF, 16, cfg.D), 8, axis=1)),
        qz_d=_c(qz_d), sc_d=_c(sc_d),
        gidx_d=_c(gi_d.reshape(cfg.KF, 128).T.astype(np.int32)),
        conv_w=_c(conv_w.reshape(cfg.NXBC, 128, cfg.CK).transpose(1, 0, 2)
                  .reshape(128, cfg.NXBC * cfg.CK)),
        conv_b=_c(conv_b.reshape(cfg.NXBC, 128).T),
        dtb=_c(dt_bias.reshape(cfg.H, 1)),
        alog=_c(a_log.reshape(cfg.H, 1)),
        dvec=_c(dvec.reshape(1, cfg.H)),
        nw=_c(norm_w.reshape(cfg.KDI, 128).T),
        **make_consts(cfg),
    )
    in_maps = []
    for core in range(cfg.n_cores):
        bb, half = core // 2, core % 2
        m = dict(shared)
        xr = x.reshape(cfg.batch, -1, cfg.D)
        m["x"] = _c(xr[bb, half * cfg.T:(half + 1) * cfg.T, :])
        m["x_other"] = _c(xr[bb, (1 - half) * cfg.T:(2 - half) * cfg.T, :])
        m["flag"] = np.full((128, 1), float(half), np.float32)
        in_maps.append(m)
    return in_maps


def build(cfg):
    nc = bass.Bass("TRN2", target_bir_lowering=False)
    es = contextlib.ExitStack()
    nc._es = es
    c = cfg

    def din(name, shape, dt):
        return nc.dram_tensor(name, list(shape), dt, kind="ExternalInput").ap()

    x_d = din("x", (c.T, c.D), F32)
    x_other_d = din("x_other", (c.T, c.D), F32)
    h_in_d = din("h_in", (c.M_IN, 128, c.KD, 128), F32)
    woutT_d = din("woutT", (c.DI, c.D), F32)
    qw_gu_d = din("qw_gu", (c.KD, 128, c.JTOT), U32)
    qz_gu_d = din("qz_gu", (c.G_GU, c.JTOT // 8), U32)
    sc_gu_d = din("sc_gu", (c.G_GU, c.JTOT), F32)
    _JSL = min(1280, c.JTOT)
    _NSL = (c.JTOT + _JSL - 1) // _JSL
    sc_gu_sl_d = din("sc_gu_sl", (_NSL * c.G_GU, _JSL), F32)
    gidx_gu_d = din("gidx_gu", (128, c.KD), I32)
    qw_d_d = din("qw_d", (c.KF, 128, c.D), U32)
    qz_d_d = din("qz_d", (c.G_D, c.D // 8), U32)
    sc_d_d = din("sc_d", (c.G_D, c.D), F32)
    gidx_d_d = din("gidx_d", (128, c.KF), I32)
    conv_w_d = din("conv_w", (128, c.NXBC * c.CK), F32)
    conv_b_d = din("conv_b", (128, c.NXBC), F32)
    dtb_d = din("dtb", (c.H, 1), F32)
    alog_d = din("alog", (c.H, 1), F32)
    dvec_d = din("dvec", (1, c.H), F32)
    nw_d = din("nw", (128, c.KDI), F32)
    flag_d = din("flag", (128, 1), F32)
    ident_f_d = din("ident_f", (128, 128), F32)
    ident_b_d = din("ident_b", (128, 128), BF16)
    i64_2_d = din("i64_2", (128, 64), BF16)
    ones1_b_d = din("ones1_b", (1, 128), BF16)
    ones1_f_d = din("ones1_f", (1, 128), F32)
    ones128_f_d = din("ones128_f", (128, 1), F32)
    onesNCH_b_d = din("onesNCH_b", (c.NCH, 128), BF16)
    masks_b_d = din("masks_b", (2, 128, c.C), BF16)
    m15_d = din("m15", (128, 1), U32)
    minv_d = din("minv", (128, 1), F32)
    iota_gu_d = din("iota_gu", (128, c.G_GU), F32)
    iota_d_d = din("iota_d", (128, c.G_D), F32)
    fvals = _fconst_vals(c)
    fconst_d = din("fconst", (128, len(fvals)), F32)

    out_d = nc.dram_tensor("out", [c.T, c.D], F32, kind="ExternalOutput").ap()

    def dscr(name, shape, dt):
        return nc.dram_tensor(name, list(shape), dt, kind="Internal").ap()

    s_xn = dscr("s_xn", (c.T, c.D), BF16)
    s_z = dscr("s_z", (c.DI, c.T), BF16)
    s_xh = dscr("s_xh", (c.DI, c.T), BF16)
    s_bc = dscr("s_bc", (2 * c.GN, c.T), BF16)
    s_wgu = dscr("s_wgu", (c.KD, 128, c.JTOT), BF16)
    s_zgu = dscr("s_zgu", (2, c.G_GU, c.JTOT), BF16)
    s_h = dscr("s_h", (c.KF, 128, c.T), BF16)
    s_ygn = dscr("s_ygn", (c.KDI, 128, c.T), BF16)
    s_cs = dscr("s_cs", (c.H, c.T), F32)
    s_srecv = dscr("s_srecv", (128, c.H * c.P), BF16)
    s_rs = dscr("s_rs", (1, c.T), F32)
    s_u = dscr("s_u", (c.T, c.D), F32)
    s_dn = dscr("s_dn", (c.T, c.D), F32)

    b = B(nc, es)
    S = b.sem
    TE, DV, AC = S("te"), S("dve"), S("act")

    def sb(st, name, shape, dt, side=None):
        return st.enter_context(nc.sbuf_tensor(name, list(shape), dt, side=side))

    def pt(st, name, shape, dt=F32):
        return st.enter_context(nc.psum_tensor(name, list(shape), dt))

    ident_f = sb(es, "c_idf", [128, 128], F32)
    ident_b = sb(es, "c_idb", [128, 128], BF16)
    i64_2 = sb(es, "c_i64", [128, 64], BF16)
    ones1_b = sb(es, "c_o1b", [1, 128], BF16)
    ones1_f = sb(es, "c_o1f", [1, 128], F32)
    ones128_f = sb(es, "c_o128f", [128, 1], F32)
    onesNCH_b = sb(es, "c_oN", [c.NCH, 128], BF16)
    masks_b = sb(es, "c_mask", [128, 2 * c.C], BF16)
    m15 = sb(es, "c_m15", [128, 1], U32)
    minv = sb(es, "c_minv", [128, 1], F32)
    iota_gu = sb(es, "c_iog", [128, c.G_GU], F32)
    iota_d = sb(es, "c_iod", [128, c.G_D], F32)
    convw = sb(es, "c_cw", [128, c.NXBC * c.CK], F32)
    convb = sb(es, "c_cb", [128, c.NXBC], F32)
    dtb = sb(es, "c_dtb", [c.H, 1], F32)
    alog = sb(es, "c_alog", [c.H, 1], F32)
    dvec_s = sb(es, "c_dv", [1, c.H], F32)
    nw_s = sb(es, "c_nw", [128, c.KDI], F32)
    flag_s = sb(es, "c_flag", [128, 1], F32)
    gidx_gu_s = sb(es, "c_gig", [128, c.KD], I32)
    gidx_d_s = sb(es, "c_gid", [128, c.KF], I32)
    fconst_s = sb(es, "c_fc", [128, len(fvals)], F32)
    for j, v in enumerate(fvals):
        nc.const_aps.register(F32, v, fconst_s[:, j:j + 1])
    halo3 = sb(es, "halo3", [128, c.NXBC * 3], F32)

    CONST = S("cst")
    for dst, src in [(ident_f, ident_f_d), (ident_b, ident_b_d),
                     (i64_2, i64_2_d), (ones1_b, ones1_b_d),
                     (ones1_f, ones1_f_d), (ones128_f, ones128_f_d),
                     (onesNCH_b, onesNCH_b_d), (m15, m15_d), (minv, minv_d),
                     (iota_gu, iota_gu_d), (iota_d, iota_d_d),
                     (convw, conv_w_d), (convb, conv_b_d),
                     (dtb, dtb_d), (alog, alog_d), (dvec_s, dvec_d),
                     (nw_s, nw_d), (flag_s, flag_d),
                     (gidx_gu_s, gidx_gu_d), (gidx_d_s, gidx_d_d),
                     (fconst_s, fconst_d)]:
        b.emit("sync", lambda e, d=dst, s=src: e.dma_start(d[:], s[:]),
               inc=CONST, n=16)
    for sh in range(2):
        b.emit("sync", lambda e, sh=sh: e.dma_start(
            masks_b[:, sh * c.C:(sh + 1) * c.C], masks_b_d[sh, :, :]),
            inc=CONST, n=16)
    ctok = (CONST, CONST.count)
    for e in ENGS:
        b.wait(e, ctok)

    NT = c.T // 512
    KZ = c.KDI
    def emit_ln(tag, xsrc):
        esA = contextlib.ExitStack()
        xa = sb(esA, f"xa{tag}", [128, c.D], F32)
        xna = sb(esA, f"xna{tag}", [128, c.D], BF16)
        stat = sb(esA, f"stat{tag}", [128, 8], F32)
        XL = S(f"xld{tag}")
        XS = S(f"xnst{tag}")
        xa_free = None
        xna_free = None
        st_free = None
        for tt in range(c.TT):
            ld = b.emit("sync", lambda e, tt=tt: e.dma_start(
                xa[:], xsrc[tt * 128:(tt + 1) * 128, :]),
                waits=[xa_free], inc=XL, n=16)
            t1 = b.emit("vector", lambda e: e.reduce_sum(
                stat[:, 0:1], xa[:], axis=AX.X), waits=[ld, st_free], inc=DV)
            t2 = b.emit("vector", lambda e: e.tensor_scalar_mul(
                stat[:, 1:2], stat[:, 0:1], 1.0 / c.D), waits=[t1], inc=DV)
            t3 = b.emit("scalar", lambda e: e.activation(
                xna[:], xa[:], AF.Square), waits=[ld, xna_free], inc=AC)
            t4 = b.emit("vector", lambda e: e.reduce_sum(
                stat[:, 2:3], xna[:], axis=AX.X), waits=[t3], inc=DV)
            t4 = b.emit("vector", lambda e: e.tensor_scalar_mul(
                stat[:, 2:3], stat[:, 2:3], 1.0 / c.D), waits=[t4], inc=DV)
            t4b = b.emit("vector", lambda e: e.tensor_scalar(
                stat[:, 5:6], stat[:, 1:2], stat[:, 1:2], None, op0=ALU.mult),
                waits=[t2, t4], inc=DV)
            t4c = b.emit("vector", lambda e: e.tensor_sub(
                stat[:, 2:3], stat[:, 2:3], stat[:, 5:6]), waits=[t4b], inc=DV)
            t5a = b.emit("scalar", lambda e: e.activation(
                stat[:, 4:5], stat[:, 2:3], AF.Sqrt,
                bias=float(c.eps_ln)), waits=[t4c], inc=AC)
            t5 = b.emit("vector", lambda e: e.reciprocal(
                stat[:, 3:4], stat[:, 4:5]), waits=[t5a], inc=DV)
            t6 = b.emit("vector", lambda e: e.tensor_scalar(
                xna[:], xa[:], stat[:, 1:2], stat[:, 3:4],
                op0=ALU.subtract, op1=ALU.mult),
                waits=[t5, t4], inc=DV)
            xa_free = t6
            st_free = t6
            xna_free = b.emit("sync", lambda e, tt=tt: e.dma_start(
                s_xn[tt * 128:(tt + 1) * 128, :], xna[:]),
                waits=[t6], inc=XS, n=16)
        b.barrier()
        b.flush()
        esA.close()

    def emit_xnfm(tag, xn_fm):
        XT = S(f"xntr{tag}")
        for k in range(c.KD):
            b.emit("sync", lambda e, k=k: e.dma_start(
                xn_fm[:, k * c.T:(k + 1) * c.T],
                s_xn[:, k * 128:(k + 1) * 128], transpose=True), inc=XT, n=16)
        b.barrier()
        b.flush()

    def emit_inproj(tag, xn_fm, dt_raw, pre):
        esB = contextlib.ExitStack()
        wb = [sb(esB, f"wb{tag}{i}", [128, c.KD * 128], BF16) for i in range(2)]
        pB = [pt(esB, f"pB{tag}{i}", [128, c.T]) for i in range(2)]
        xbcb = sb(esB, f"xbcb{tag}", [128, c.T + 3], F32)
        cacc = sb(esB, f"cacc{tag}", [128, c.T], F32)
        ev_c = sb(esB, f"ev_c{tag}", [128, c.T], BF16)
        WL = [S(f"winld{tag}0"), S(f"winld{tag}1")]
        ZSTs, XHST, BCST = S(f"zst{tag}"), S(f"xhst{tag}"), S(f"bcst{tag}")
        wb_free = [None, None]
        pB_free = [None, None]
        ev_z_free = None
        ev_c_free = None
        xbc_free = []
        cacc_free = []
        halo_free = []
        if pre:
            z0 = b.emit("vector", lambda e: e.memset(xbcb[:, 0:3], 0.0), inc=DV)
        ms = list(range(KZ, c.M_IN)) if pre else list(range(c.M_IN))
        i = 0
        for m in ms:
            j = m - KZ
            halo_only = pre and (KZ + c.KDI + c.NB <= m < KZ + c.NXBC)
            rows = min(128, c.d_in - m * 128)
            wtok = b.emit("gpsimd", lambda e, i=i, m=m: e.dma_start(
                wb[i][:], h_in_d[m, :, :, :].rearrange("p k c -> p (k c)")),
                waits=[wb_free[i]], inc=WL[i], n=16)
            mmtok = None
            if halo_only:
                for k in range(c.KD):
                    mmtok = b.emit("tensor", lambda e, i=i, k=k: e.matmul(
                        pB[i][0:128, 0:3],
                        wb[i][:, k * 128:(k + 1) * 128],
                        xn_fm[:, k * c.T + c.T - 3:k * c.T + c.T],
                        start=(k == 0), stop=(k == c.KD - 1)),
                        waits=([wtok, pB_free[i]] if k == 0 else []),
                        inc=(TE if k == c.KD - 1 else None))
                wb_free[i] = mmtok
                hv = b.emit("vector", lambda e, i=i, j=j: e.tensor_scalar(
                    halo3[:, j * 3:(j + 1) * 3], pB[i][0:128, 0:3],
                    flag_s[:, 0:1], None, op0=ALU.mult),
                    waits=[mmtok] + halo_free, inc=DV)
                halo_free = []
                pB_free[i] = hv
                i = 1 - i
                continue
            for k in range(c.KD):
                for nb in range(NT):
                    mmtok = b.emit("tensor", lambda e, i=i, k=k, nb=nb, rows=rows: e.matmul(
                        pB[i][0:rows, nb * 512:(nb + 1) * 512],
                        wb[i][:, k * 128:k * 128 + rows],
                        xn_fm[:, k * c.T + nb * 512:k * c.T + (nb + 1) * 512],
                        start=(k == 0), stop=(k == c.KD - 1)),
                        waits=([wtok, pB_free[i]] if k == 0 and nb == 0 else []),
                        inc=(TE if (k == c.KD - 1 and nb == NT - 1) else None))
            wb_free[i] = mmtok
            if m < KZ:
                evv = b.emit("scalar", lambda e, i=i: e.activation(
                    ev_c[:], pB[i][:], AF.Silu), waits=[mmtok, ev_c_free], inc=AC)
                pB_free[i] = evv
                ev_c_free = b.emit("sync", lambda e, m=m: e.dma_start(
                    s_z[m * 128:(m + 1) * 128, :], ev_c[:]),
                    waits=[evv], inc=ZSTs, n=16)
            elif m < KZ + c.NXBC:
                evv = b.emit("scalar", lambda e, i=i: e.copy(
                    xbcb[:, 3:3 + c.T], pB[i][:]), waits=[mmtok] + xbc_free, inc=AC)
                pB_free[i] = evv
                if pre:
                    hv = b.emit("vector", lambda e, j=j: e.tensor_scalar(
                        halo3[:, j * 3:(j + 1) * 3], xbcb[:, c.T:c.T + 3],
                        flag_s[:, 0:1], None, op0=ALU.mult),
                        waits=[evv] + halo_free, inc=DV)
                    halo_free = []
                    h0tok = z0
                else:
                    hv = None
                    h0tok = b.emit("vector", lambda e, j=j: e.tensor_copy(
                        xbcb[:, 0:3], halo3[:, j * 3:(j + 1) * 3]),
                        waits=[evv] + xbc_free, inc=DV)
                t = b.emit("vector", lambda e, j=j: e.tensor_scalar_mul(
                    cacc[:], xbcb[:, 0:c.T], convw[:, j * c.CK:j * c.CK + 1]),
                    waits=[evv, h0tok] + cacc_free, inc=DV)
                for kk in range(1, c.CK):
                    t = b.emit("vector", lambda e, j=j, kk=kk: e.scalar_tensor_tensor(
                        cacc[:], xbcb[:, kk:kk + c.T],
                        convw[:, j * c.CK + kk:j * c.CK + kk + 1],
                        cacc[:], op0=ALU.mult, op1=ALU.add), waits=[t], inc=DV)
                xbc_free = [t] + ([hv] if hv else [])
                evc = b.emit("scalar", lambda e, j=j: e.activation(
                    ev_c[:], cacc[:], AF.Silu, bias=convb[:, j:j + 1]),
                    waits=[t, ev_c_free], inc=AC)
                cacc_free = [evc]
                if j < c.KDI:
                    ev_c_free = b.emit("sync", lambda e, j=j: e.dma_start(
                        s_xh[j * 128:(j + 1) * 128, :], ev_c[:]),
                        waits=[evc], inc=XHST, n=16)
                else:
                    jj = j - c.KDI
                    ev_c_free = b.emit("sync", lambda e, jj=jj: e.dma_start(
                        s_bc[jj * 128:(jj + 1) * 128, :], ev_c[:]),
                        waits=[evc], inc=BCST, n=16)
            else:
                evv = b.emit("scalar", lambda e, i=i, rows=rows: e.copy(
                    dt_raw[0:rows, :], pB[i][0:rows, :]), waits=[mmtok], inc=AC)
                pB_free[i] = evv
            i = 1 - i
        b.barrier()
        b.flush()
        esB.close()

    def emit_dtcs(tag, dt_raw, need_cumE, store_cs=False):
        es_dt = contextlib.ExitStack()
        csb = sb(es_dt, f"csb{tag}", [c.H, c.T], F32)
        acol = sb(es_dt, f"acol{tag}", [c.H, 1], F32)
        t_all = sb(es_dt, f"t_all{tag}", [c.H, c.NCH], F32)
        expT = sb(es_dt, f"expT{tag}", [c.H, c.NCH], F32)
        cumT = sb(es_dt, f"cumT{tag}", [c.H, c.NCH], F32)
        pT = pt(es_dt, f"pT{tag}", [128, 512])
        pT2 = pt(es_dt, f"pT2{tag}", [128, 512])
        spv = sb(es_dt, f"spv{tag}", [c.H, c.T], F32)
        spa = sb(es_dt, f"spa{tag}", [c.H, c.T], F32)
        rowb = sb(es_dt, f"rowb{tag}", [1, c.H], F32)
        t = b.emit("vector", lambda e: e.tensor_scalar_add(
            spv[:], dt_raw[:], dtb[:]), inc=DV)
        t1 = b.emit("scalar", lambda e: e.activation(spa[:], spv[:], AF.Abs),
                    waits=[t], inc=AC)
        t1 = b.emit("scalar", lambda e: e.activation(
            spa[:], spa[:], AF.Exp, scale=-1.0), waits=[t1], inc=AC)
        t1 = b.emit("vector", lambda e: e.tensor_scalar_add(spa[:], spa[:], 1.0),
                    waits=[t1], inc=DV)
        t1 = b.emit("scalar", lambda e: e.activation(spa[:], spa[:], AF.Ln),
                    waits=[t1], inc=AC)
        t2 = b.emit("scalar", lambda e: e.activation(dt_f[:], spv[:], AF.Relu),
                    waits=[t, t1], inc=AC)
        t = b.emit("vector", lambda e: e.tensor_add(dt_f[:], dt_f[:], spa[:]),
                   waits=[t1, t2], inc=DV)
        ta = b.emit("scalar", lambda e: e.activation(acol[:], alog[:], AF.Exp),
                    inc=AC)
        ta = b.emit("vector", lambda e: e.tensor_scalar_mul(acol[:], acol[:], -1.0),
                    waits=[ta], inc=DV)
        t = b.emit("vector", lambda e: e.tensor_scalar_mul(cs[:], dt_f[:], acol[:]),
                   waits=[t, ta], inc=DV)
        srcb, dstb = cs, csb
        step = 1
        while step < c.C:
            last = None
            for ch in range(c.NCH):
                o = ch * c.C
                b.emit("vector", lambda e, s=srcb, d=dstb, o=o, st=step: e.tensor_copy(
                    d[:, o:o + st], s[:, o:o + st]), waits=[t], inc=DV)
                last = b.emit("vector", lambda e, s=srcb, d=dstb, o=o, st=step: e.tensor_add(
                    d[:, o + st:o + c.C], s[:, o + st:o + c.C], s[:, o:o + c.C - st]),
                    waits=[t], inc=DV)
            t = last
            srcb, dstb = dstb, srcb
            step *= 2
        if srcb is not cs:
            t = b.emit("vector", lambda e: e.tensor_copy(cs[:], csb[:]),
                       waits=[t], inc=DV)
        cs_tok = t
        cs_st_tok = None
        if store_cs:
            CSST = S(f"csst{tag}")
            cs_st_tok = b.emit("sync", lambda e: e.dma_start(s_cs[:], cs[:]),
                               waits=[cs_tok], inc=CSST, n=16)
        tt_toks = []
        for ch in range(c.NCH):
            tt_toks.append(b.emit("vector", lambda e, ch=ch: e.tensor_copy(
                t_all[:, ch:ch + 1], cs[:, ch * c.C + c.C - 1:ch * c.C + c.C]),
                waits=[cs_tok], inc=DV))
        t = b.emit("scalar", lambda e: e.activation(expT[:], t_all[:], AF.Exp),
                   waits=tt_toks, inc=AC)

        def bcast_cols(src_fm, dest_sb, wtok):
            last = None
            prev_cp = None
            prev_mm = None
            for ch in range(c.NCH):
                tr = b.emit("tensor", lambda e, ch=ch, s=src_fm: e.transpose(
                    pT[0:1, 0:c.H], s[:, ch:ch + 1], ident_f[0:c.H, 0:c.H]),
                    waits=wtok + [prev_cp], inc=TE)
                cp = b.emit("scalar", lambda e: e.copy(
                    rowb[0:1, 0:c.H], pT[0:1, 0:c.H]), waits=[tr, prev_mm], inc=AC)
                prev_cp = cp
                mm = b.emit("tensor", lambda e: e.matmul(
                    pT2[0:128, 0:c.H], ones1_f[0:1, :], rowb[0:1, 0:c.H]),
                    waits=[cp, last], inc=TE)
                prev_mm = mm
                last = b.emit("vector", lambda e, ch=ch, d=dest_sb: e.tensor_copy(
                    d[:, ch * c.H:(ch + 1) * c.H], pT2[0:128, 0:c.H]),
                    waits=[mm], inc=DV)
            return last

        ea_tok = bcast_cols(expT, e_all, [t])
        cumE_tok = None
        if need_cumE:
            prev = b.emit("vector", lambda e: e.memset(cumT[:, 0:1], 0.0), inc=DV)
            for ch in range(1, c.NCH):
                prev = b.emit("vector", lambda e, ch=ch: e.tensor_add(
                    cumT[:, ch:ch + 1], cumT[:, ch - 1:ch], t_all[:, ch - 1:ch]),
                    waits=[prev] + tt_toks, inc=DV)
            tcE = b.emit("scalar", lambda e: e.activation(cumT[:], cumT[:], AF.Exp),
                         waits=[prev], inc=AC)
            ce_tok = bcast_cols(cumT, cumE_all, [tcE, ea_tok])
            cumE_tok = b.emit("vector", lambda e: e.tensor_scalar_mul(
                cumE_all[:], cumE_all[:], flag_s[:, 0:1]), waits=[ce_tok], inc=DV)
        b.barrier()
        b.flush()
        es_dt.close()
        return cs_tok, cs_st_tok, ea_tok, cumE_tok

    def make_ssd_ctx(st, tag, ptr=None):
        d = {}
        d["xtk"] = [[sb(st, f"xtk{tag}{sh}", [128, c.DI], BF16) for sh in range(2)]]
        d["bcb"] = [sb(st, f"bcb{tag}", [128, 2 * c.NB * c.C], BF16)]
        d["cstk"] = sb(st, f"cstk{tag}", [128, 2 * c.H], F32)
        d["ncstk"] = sb(st, f"ncstk{tag}", [128, 2 * c.H], F32)
        d["dttk"] = sb(st, f"dttk{tag}", [128, 2 * c.H], F32)
        d["ddmf"] = sb(st, f"ddmf{tag}", [c.H, c.C], F32)
        d["ddmtk"] = sb(st, f"ddmtk{tag}", [128, 2 * c.H], F32)
        d["btokb"] = sb(st, f"btokb{tag}", [128, 256], BF16)
        d["bdb"] = sb(st, f"bdb{tag}", [128, 512], BF16)
        d["S_a"] = sb(st, f"S_a{tag}", [128, c.H * c.P], F32)
        d["pS"] = [pt(st, f"pS{tag}{i}", [128, 512]) for i in range(2)]
        d["pTR"] = ptr if ptr is not None else pt(st, f"pTR{tag}", [128, 512])
        d["pTRB"] = pt(st, f"pTRB{tag}", [128, 128], BF16)
        d["XTK"] = [S(f"xtl{tag}0"), S(f"xtl{tag}1")]
        d["BCL"] = S(f"bcl{tag}")
        d["xtk_free"] = [None, None]
        d["bcb_free"] = [None]
        d["ptr_free"] = [None]
        d["ptrb_free"] = [None]
        d["ddmf_free"] = [[]]

        def load_chunk(ch, b_only=False):
            o = ch * c.C
            xts = [b.emit("sync", lambda e, sh=sh, o=o: e.dma_start(
                d["xtk"][0][sh][:], s_xh[:, o + sh * 128:o + (sh + 1) * 128],
                transpose=True), waits=[d["xtk_free"][sh]], inc=d["XTK"][sh], n=16)
                for sh in range(2)]
            bt = None
            for g in range(c.NB if b_only else 2 * c.NB):
                bt = b.emit("sync", lambda e, g=g, o=o: e.dma_start(
                    d["bcb"][0][:, g * c.C:(g + 1) * c.C],
                    s_bc[g * 128:(g + 1) * 128, o:o + c.C]),
                    waits=[d["bcb_free"][0]] if g == 0 else [], inc=d["BCL"], n=16)
            return xts, [bt]

        def chunk_trans(ch, waits):
            o = ch * c.C
            out = []
            t1 = b.emit("scalar", lambda e, o=o: e.activation(
                d["ddmf"][:], cs[:, o:o + c.C], AF.Exp,
                bias=cs[:, o + c.C - 1:o + c.C], scale=-1.0),
                waits=waits + d["ddmf_free"][0], inc=AC)
            t1 = b.emit("vector", lambda e, o=o: e.tensor_mul(
                d["ddmf"][:], d["ddmf"][:], dt_f[:, o:o + c.C]), waits=[t1], inc=DV)
            ddmf_readers = [t1]
            for sh in range(2):
                oo = o + sh * 128
                a1 = b.emit("tensor", lambda e, oo=oo: e.transpose(
                    d["pTR"][:, 0:c.H], cs[:, oo:oo + 128], ident_f[0:c.H, 0:c.H]),
                    waits=waits + [d["ptr_free"][0]], inc=TE)
                a2 = b.emit("vector", lambda e, sh=sh: e.tensor_copy(
                    d["cstk"][:, sh * c.H:(sh + 1) * c.H], d["pTR"][0:128, 0:c.H]),
                    waits=[a1], inc=DV)
                a3 = b.emit("vector", lambda e, sh=sh: e.tensor_scalar_mul(
                    d["ncstk"][:, sh * c.H:(sh + 1) * c.H],
                    d["cstk"][:, sh * c.H:(sh + 1) * c.H], -1.0), waits=[a2], inc=DV)
                a4 = b.emit("tensor", lambda e, oo=oo: e.transpose(
                    d["pTR"][:, 128:128 + c.H], dt_f[:, oo:oo + 128],
                    ident_f[0:c.H, 0:c.H]), waits=[a2], inc=TE)
                a5 = b.emit("vector", lambda e, sh=sh: e.tensor_copy(
                    d["dttk"][:, sh * c.H:(sh + 1) * c.H],
                    d["pTR"][0:128, 128:128 + c.H]), waits=[a4], inc=DV)
                a6 = b.emit("tensor", lambda e, sh=sh: e.transpose(
                    d["pTR"][:, 256:256 + c.H], d["ddmf"][:, sh * 128:(sh + 1) * 128],
                    ident_f[0:c.H, 0:c.H]), waits=[t1, a5], inc=TE)
                ddmf_readers.append(a6)
                a7 = b.emit("vector", lambda e, sh=sh: e.tensor_copy(
                    d["ddmtk"][:, sh * c.H:(sh + 1) * c.H],
                    d["pTR"][0:128, 256:256 + c.H]), waits=[a6], inc=DV)
                out += [a3, a5, a7]
            d["ptr_free"][0] = out[-1]
            d["ddmf_free"][0] = ddmf_readers
            return out

        d["load_chunk"] = load_chunk
        d["chunk_trans"] = chunk_trans
        return d

    def emit_scan(ctx, tag, cs_tok, ea_tok, store_states):
        S_a = ctx["S_a"]
        pS = ctx["pS"][0]
        st_stage = None
        SST = None
        if store_states:
            st_stage = ctx["st_stage"]
            SST = S(f"stst{tag}")
        tz = b.emit("vector", lambda e: e.memset(S_a[:], 0.0), inc=DV)
        s_cur_tok = tz
        pS_free = []
        st_w_free = None
        bd_free = []
        for ch in range(c.NCH):
            xts, bts = ctx["load_chunk"](ch, b_only=not store_states)
            ctoks = ctx["chunk_trans"](ch, [cs_tok])
            st_rd = []
            for g in range(c.G):
                gB = g * (c.NST // 128)
                trts = []
                for sh in range(2):
                    u1 = b.emit("tensor", lambda e, gB=gB, sh=sh: e.transpose(
                        ctx["pTRB"][:, 0:128],
                        ctx["bcb"][0][:, gB * c.C + sh * 128:gB * c.C + (sh + 1) * 128],
                        ident_b[:]), waits=bts + st_rd + [ctx["ptrb_free"][0]], inc=TE)
                    u2 = b.emit("vector", lambda e, sh=sh: e.tensor_copy(
                        ctx["btokb"][:, sh * 128:(sh + 1) * 128],
                        ctx["pTRB"][0:128, 0:128]), waits=[u1], inc=DV)
                    ctx["ptrb_free"][0] = u2
                    trts.append(u2)
                for hh in range(c.H // c.G):
                    h = g * (c.H // c.G) + hh
                    v1 = b.emit("vector", lambda e, h=h: e.tensor_scalar(
                        ctx["bdb"][:, 0:128], ctx["btokb"][:, 0:128],
                        ctx["ddmtk"][:, h:h + 1], None, op0=ALU.mult),
                        waits=trts + ctoks + bd_free, inc=DV)
                    v2 = b.emit("vector", lambda e, h=h: e.tensor_scalar(
                        ctx["bdb"][:, 128:256], ctx["btokb"][:, 128:256],
                        ctx["ddmtk"][:, c.H + h:c.H + h + 1], None, op0=ALU.mult),
                        waits=trts + ctoks + bd_free, inc=DV)
                    m1 = b.emit("tensor", lambda e, h=h: e.matmul(
                        pS[:, 0:c.P], ctx["bdb"][:, 0:128],
                        ctx["xtk"][0][0][:, h * c.P:(h + 1) * c.P],
                        start=True, stop=False),
                        waits=[v1] + xts + pS_free, inc=None)
                    m2 = b.emit("tensor", lambda e, h=h: e.matmul(
                        pS[:, 0:c.P], ctx["bdb"][:, 128:256],
                        ctx["xtk"][0][1][:, h * c.P:(h + 1) * c.P],
                        start=False, stop=True), waits=[v2], inc=TE)
                    bd_free = [m2]
                    e2 = b.emit("vector", lambda e, h=h, ch=ch: e.scalar_tensor_tensor(
                        S_a[:, h * c.P:(h + 1) * c.P],
                        S_a[:, h * c.P:(h + 1) * c.P],
                        e_all[:, ch * c.H + h:ch * c.H + h + 1],
                        pS[:, 0:c.P], op0=ALU.mult, op1=ALU.add),
                        waits=[m2, s_cur_tok, ea_tok], inc=DV)
                    pS_free = [e2]
                    st_rd.append(e2)
                    if store_states:
                        e1 = b.emit("scalar", lambda e, h=h: e.copy(
                            st_stage[:, h * c.P:(h + 1) * c.P],
                            S_a[:, h * c.P:(h + 1) * c.P]),
                            waits=[e2, st_w_free], inc=AC)
                        st_rd.append(e1)
            if store_states:
                stt = b.emit("sync", lambda e, ch=ch: e.dma_start(
                    s_states[ch, :, :], st_stage[:]), waits=st_rd, inc=SST, n=16)
                st_w_free = stt
                gate = stt
            else:
                gate = st_rd[-1]
                for tk in st_rd:
                    pass
            for sh in range(2):
                ctx["xtk_free"][sh] = gate
            ctx["bcb_free"][0] = gate
            s_cur_tok = gate if store_states else st_rd[-1]
        return s_cur_tok, S_a

    esP2 = contextlib.ExitStack()
    dt_raw_p = sb(esP2, "dt_raw_p", [c.H, c.T], F32, side="right")
    esP1 = contextlib.ExitStack()
    xn_fm_p = sb(esP1, "xn_fm_p", [128, c.KD * c.T], BF16, side="right")
    emit_ln("p", x_other_d)
    emit_xnfm("p", xn_fm_p)
    emit_inproj("p", xn_fm_p, dt_raw_p, pre=True)
    esP1.close()
    es_mid = contextlib.ExitStack()
    dt_f = sb(es_mid, "dt_f", [c.H, c.T], F32)
    cs = sb(es_mid, "cs", [c.H, c.T], F32)
    e_all = sb(es_mid, "e_all", [128, c.NCH * c.H], F32)
    cumE_all = sb(es_mid, "cumE", [128, c.NCH * c.H], F32)
    cs_tok_p, _, ea_tok_p, _ = emit_dtcs("p", dt_raw_p, need_cumE=False)
    esP2.close()
    esPS = contextlib.ExitStack()
    ctx_p = make_ssd_ctx(esPS, "p")
    sr_stg = sb(esPS, "sr_stg", [128, c.H * c.P], BF16)
    sfin_tok, Sfin = emit_scan(ctx_p, "p", cs_tok_p, ea_tok_p, store_states=False)
    srt = b.emit("vector", lambda e: e.tensor_scalar(
        sr_stg[:], Sfin[:], flag_s[:, 0:1], None, op0=ALU.mult),
        waits=[sfin_tok], inc=DV)
    SRST = S("srst")
    b.emit("sync", lambda e: e.dma_start(s_srecv[:], sr_stg[:]),
           waits=[srt], inc=SRST, n=16)
    b.barrier()
    b.flush()
    esPS.close()

    es_xn = contextlib.ExitStack()
    xn_fm = sb(es_xn, "xn_fm", [128, c.KD * c.T], BF16, side="right")
    es_dtr = contextlib.ExitStack()
    dt_raw = sb(es_dtr, "dt_raw", [c.H, c.T], F32, side="right")
    emit_ln("m", x_d)
    emit_xnfm("m", xn_fm)
    emit_inproj("m", xn_fm, dt_raw, pre=False)
    cs_tok, cs_st_tok, ea_tok, cumE_tok = emit_dtcs("m", dt_raw, need_cumE=False,
                                                     store_cs=True)
    es_dtr.close()
    esG2 = contextlib.ExitStack()
    xg_hi = sb(esG2, "xg_hi", [c.G_GU, c.T], BF16)
    xg_lo = sb(esG2, "xg_lo", [c.G_GU, c.T], BF16)
    esG2a = contextlib.ExitStack()
    ZSL = min(1024, c.JTOT)
    zslabs = []
    o = 0
    while o < c.JTOT:
        w = min(ZSL, c.JTOT - o)
        zslabs.append((o, w))
        o += w
    zq_s = sb(esG2a, "zqs", [c.G_GU, ZSL // 8], U32)
    zf = sb(esG2a, "zf", [c.G_GU, ZSL], F32)
    ztu = sb(esG2a, "ztu", [c.G_GU, ZSL // 8], U32)
    ztf = sb(esG2a, "ztf", [c.G_GU, ZSL // 8], F32)
    scz = sb(esG2a, "scz", [c.G_GU, ZSL], F32)
    znst = sb(esG2a, "znst", [c.G_GU, ZSL], BF16)
    oh = sb(esG2a, "oh", [128, c.KD * c.G_GU], BF16)
    gxf = sb(esG2a, "gxf", [128, c.KD], F32)
    xg_f = sb(esG2a, "xg_f", [c.G_GU, c.T], F32)
    pX = pt(esG2a, "pX", [c.G_GU, c.T])
    ZL = S("zl")
    ZQL = S("zql")
    ZST2 = S("zst2")
    zn_st_last = None
    prevs = []
    zq_prev = []
    for (o, w) in zslabs:
        zqt = b.emit("sync", lambda e, o=o, w=w: e.dma_start(
            zq_s[:, 0:w // 8], qz_gu_d[:, o // 8:(o + w) // 8]),
            waits=zq_prev, inc=ZQL, n=16)
        lsc = b.emit("sync", lambda e, o=o, w=w: e.dma_start(
            scz[:, 0:w], sc_gu_d[:, o:o + w]), waits=prevs, inc=ZL, n=16)
        ztoks = []
        for sft in range(8):
            u1 = b.emit("vector", lambda e, o=o, w=w, sft=sft: e.tensor_scalar(
                ztu[:, 0:w // 8], zq_s[:, 0:w // 8],
                (15 << (4 * sft)), None, op0=ALU.bitwise_and),
                waits=[zqt] + ztoks[-3:] + prevs, inc=DV)
            u2 = b.emit("vector", lambda e, w=w: e.tensor_copy(
                ztf[:, 0:w // 8], ztu[:, 0:w // 8]), waits=[u1], inc=DV)
            u3 = b.emit("vector", lambda e, w=w, sft=sft: e.tensor_scalar_mul(
                zf[:, sft:w:8], ztf[:, 0:w // 8],
                float(2.0 ** (-4 * sft))), waits=[u2], inc=DV)
            ztoks += [u1, u2, u3]
        t = b.emit("vector", lambda e, w=w: e.tensor_scalar_add(
            zf[:, 0:w], zf[:, 0:w], 1.0), waits=ztoks, inc=DV)
        t = b.emit("vector", lambda e, w=w: e.tensor_mul(
            zf[:, 0:w], zf[:, 0:w], scz[:, 0:w]), waits=[t, lsc], inc=DV)
        t = b.emit("vector", lambda e, w=w: e.tensor_scalar_mul(
            zf[:, 0:w], zf[:, 0:w], -1.0), waits=[t], inc=DV)
        th = b.emit("vector", lambda e, w=w: e.tensor_copy(
            znst[:, 0:w], zf[:, 0:w]), waits=[t, zn_st_last], inc=DV)
        st1 = b.emit("sync", lambda e, o=o, w=w: e.dma_start(
            s_zgu[0, :, o:o + w], znst[:, 0:w]), waits=[th], inc=ZST2, n=16)
        t2 = b.emit("vector", lambda e, w=w: e.tensor_sub(
            scz[:, 0:w], zf[:, 0:w], znst[:, 0:w]), waits=[th], inc=DV)
        tl = b.emit("vector", lambda e, w=w: e.tensor_copy(
            znst[:, 0:w], scz[:, 0:w]), waits=[t2, st1], inc=DV)
        st2 = b.emit("sync", lambda e, o=o, w=w: e.dma_start(
            s_zgu[1, :, o:o + w], znst[:, 0:w]), waits=[tl], inc=ZST2, n=16)
        zn_st_last = st2
        prevs = [t2, st2]
        zq_prev = ztoks[-3:-2]
    zn_tok = zn_st_last
    tg = b.emit("vector", lambda e: e.tensor_copy(gxf[:], gidx_gu_s[:]), inc=DV)
    ohtoks = []
    mmt = None
    for k in range(c.KD):
        otk = b.emit("vector", lambda e, k=k: e.tensor_scalar(
            oh[:, k * c.G_GU:(k + 1) * c.G_GU], iota_gu[:], gxf[:, k:k + 1],
            None, op0=ALU.is_equal), waits=[tg], inc=DV)
        ohtoks.append(otk)
    for k in range(c.KD):
        for nb in range(NT):
            mmt = b.emit("tensor", lambda e, k=k, nb=nb: e.matmul(
                pX[:, nb * 512:(nb + 1) * 512],
                oh[:, k * c.G_GU:(k + 1) * c.G_GU],
                xn_fm[:, k * c.T + nb * 512:k * c.T + (nb + 1) * 512],
                start=(k == 0), stop=(k == c.KD - 1)),
                waits=[ohtoks[k]], inc=(TE if k == c.KD - 1 and nb == NT - 1 else None))
    t = b.emit("vector", lambda e: e.tensor_copy(xg_f[:], pX[:]), waits=[mmt], inc=DV)
    t = b.emit("vector", lambda e: e.tensor_copy(xg_hi[:], xg_f[:]), waits=[t], inc=DV)
    t2 = b.emit("vector", lambda e: e.tensor_sub(xg_f[:], xg_f[:], xg_hi[:]),
                waits=[t], inc=DV)
    xg_tok = b.emit("vector", lambda e: e.tensor_copy(xg_lo[:], xg_f[:]),
                    waits=[t2], inc=DV)
    b.barrier()
    b.flush()
    esG2a.close()
    es_xn.close()

    esG1 = contextlib.ExitStack()
    JSLAB = min(1280, c.JTOT)
    slabs = []
    o = 0
    while o < c.JTOT:
        w = min(JSLAB, c.JTOT - o)
        slabs.append((o, w))
        o += w
    qwb = [sb(esG1, f"qwb{i}", [128, JSLAB], U32) for i in range(2)]
    scb = [sb(esG1, f"scb{i}", [128, JSLAB], BF16) for i in range(2)]
    nib = [sb(esG1, f"nib{i}", [128, JSLAB], BF16) for i in range(2)]
    wsl = [sb(esG1, f"wsl{i}", [128, JSLAB], BF16) for i in range(2)]
    QWL = [S("qwl0"), S("qwl1")]
    SCL = [S("scl0"), S("scl1")]
    WST = [S("wst0"), S("wst1")]
    PL_G1 = S("plg1")
    g1_state = dict(qw_free=[None, None], sc_free=[None, None],
                    wsl_free=[None, None], nib_free=[None, None], u=0)

    g1_units = [(k, si, o, w) for k in range(c.KD)
                for si, (o, w) in enumerate(slabs)]
    g1_pos = [0]
    g1_lpos = [0]
    g1_ld = {}

    def g1_load(j):
        k, si, o, w = g1_units[j]
        i = j % 2
        lq = b.emit("gpsimd", lambda e, i=i, k=k, o=o, w=w: e.dma_start(
            qwb[i][:, 0:w], qw_gu_d[k, :, o:o + w]),
            waits=[g1_state["qw_free"][i]], inc=QWL[i], n=16)
        ls = b.emit("gpsimd", lambda e, i=i, k=k, si=si: e.indirect_dma_start(
            scb[i][:], None, sc_gu_sl_d[:, :],
            IOA(gidx_gu_s[:, k:k + 1], 0),
            element_offset=si * c.G_GU * _JSL),
            waits=[g1_state["sc_free"][i]], inc=SCL[i], n=16)
        g1_ld[j] = (lq, ls)

    def g1_comp(j):
        k, si, o, w = g1_units[j]
        i = j % 2
        lq, ls = g1_ld.pop(j)
        v1 = b.emit("vector", lambda e, i=i, w=w: e.tensor_scalar(
            qwb[i][:, 0:w], qwb[i][:, 0:w], m15[:], None,
            op0=ALU.bitwise_and), waits=[lq], inc=DV)
        v2 = b.emit("scalar", lambda e, i=i, w=w: e.copy(
            nib[i][:, 0:w], qwb[i][:, 0:w]),
            waits=[v1, g1_state["nib_free"][i]], inc=AC)
        g1_state["qw_free"][i] = v2
        v5 = b.emit("vector", lambda e, i=i, w=w: e.scalar_tensor_tensor(
            wsl[i][:, 0:w], nib[i][:, 0:w], minv[:], scb[i][:, 0:w],
            op0=ALU.mult, op1=ALU.mult),
            waits=[v2, ls, g1_state["wsl_free"][i]], inc=DV)
        g1_state["nib_free"][i] = v5
        g1_state["sc_free"][i] = v5
        g1_state["wsl_free"][i] = b.emit("sync",
            lambda e, i=i, k=k, o=o, w=w: e.dma_start(
                s_wgu[k, :, o:o + w], wsl[i][:, 0:w]), waits=[v5],
            inc=WST[i], n=16)

    def g1_run(n):
        end = min(len(g1_units), g1_pos[0] + n)
        for j in range(g1_pos[0], end):
            while g1_lpos[0] < min(len(g1_units), j + 2):
                g1_load(g1_lpos[0])
                g1_lpos[0] += 1
            g1_comp(j)
        g1_pos[0] = end

    es_ssd = contextlib.ExitStack()
    pSC = pt(es_ssd, "pSC", [128, 512])
    ctx_m = make_ssd_ctx(es_ssd, "m", ptr=pSC)
    xtk = ctx_m["xtk"]
    bcb = ctx_m["bcb"]
    cstk, ncstk = ctx_m["cstk"], ctx_m["ncstk"]
    dttk, ddmtk = ctx_m["dttk"], ctx_m["ddmtk"]
    btokb, bdb = ctx_m["btokb"], ctx_m["bdb"]
    S_f32 = ctx_m["S_a"]
    pS = ctx_m["pS"]
    load_chunk0 = ctx_m["load_chunk"]
    chunk_trans0 = ctx_m["chunk_trans"]
    xtk_free = ctx_m["xtk_free"]
    bcb_free = ctx_m["bcb_free"]

    esM = contextlib.ExitStack()
    S_recv = sb(esM, "S_recv", [128, c.H * c.P], BF16)
    srun_bf = sb(esM, "srun_bf", [128, c.H * c.P], BF16)
    zjit = [sb(esM, f"zjit{i}", [128, c.C], BF16) for i in range(2)]
    xjit = [sb(esM, f"xjit{i}", [128, c.C], BF16) for i in range(2)]
    CSG = min(2, c.H // c.G)
    csrow = [sb(esM, f"csrow{i}", [1, CSG * c.C], F32) for i in range(2)]
    etmp = [sb(esM, f"etmp{i}", [128, 2 * c.C], F32) for i in range(2)]
    scsb = sb(esM, "scsb", [128, 2 * c.C], BF16)
    ebuf = [sb(esM, f"ebuf{i}", [128, 2 * c.C], BF16) for i in range(2)]
    odb = [sb(esM, f"odb{i}", [128, c.C], BF16) for i in range(2)]
    cdb = [sb(esM, f"cdb{i}", [128, c.C], BF16) for i in range(2)]
    gtb = [sb(esM, f"gtb{i}", [128, 2 * c.C], BF16) for i in range(2)]
    ygb = [sb(esM, f"ygb{i}", [128, c.C], BF16) for i in range(2)]
    ygnb = [sb(esM, f"ygnb{i}", [128, c.C], BF16) for i in range(2)]
    sqb = [sb(esM, f"sqb{i}", [128, c.C], F32) for i in range(2)]
    rmsacc = sb(esM, "rmsacc", [128, c.C], F32)
    rs_piece = sb(esM, "rs_piece", [1, c.C], F32)
    RSST = S("rsst")
    dball_all = sb(esM, "dball_all", [128, (c.H // 2) * c.P], BF16)
    dbc = sb(esM, "dbc", [128, c.H], F32)
    dvb = sb(esM, "dvb", [1, c.H], BF16)
    mng = sb(esM, "mng", [128, 2 * c.C], BF16)
    pCS = [pt(esM, f"pCS{i}", [128, 512]) for i in range(2)]
    pY = [pt(esM, f"pY{i}", [128, 512]) for i in range(2)]
    ZJ = [S(f"zj{q}") for q in range(2)]
    XJ = [S(f"xj{q}") for q in range(2)]
    YGST = [S("ygst0"), S("ygst1")]
    CSR = [S("csr0"), S("csr1")]

    mng_tok = b.emit("vector", lambda e: e.tensor_scalar(
        mng[:], masks_b[:], -1.0, 1.0e9, op0=ALU.add, op1=ALU.mult), inc=DV)
    RB = S("rcv")
    srecv_tok = b.emit("sync", lambda e: e.dma_start(S_recv[:], s_srecv[:]),
                       inc=RB, n=16)
    sinit_tok = b.emit("vector", lambda e: e.tensor_copy(S_f32[:], S_recv[:]),
                       waits=[srecv_tok], inc=DV)
    t = b.emit("vector", lambda e: e.tensor_copy(dvb[:], dvec_s[:]), inc=DV)
    t = b.emit("tensor", lambda e: e.matmul(pCS[0][:, 0:c.H], ones1_b[:], dvb[:]),
               waits=[t], inc=TE)
    t = b.emit("vector", lambda e: e.tensor_copy(dbc[:], pCS[0][:, 0:c.H]),
               waits=[t], inc=DV)
    dbc_tok = t
    DBA = S("dba")
    dba_tok = None
    for tdb in range(c.H // 2):
        dba_tok = b.emit("vector", lambda e, t=tdb: e.tensor_scalar(
            dball_all[0:64, t * c.P:(t + 1) * c.P], i64_2[0:64, :],
            dbc[0:64, 2 * t:2 * t + 1], None, op0=ALU.mult),
            waits=[dbc_tok], inc=DBA)
        dba_tok = b.emit("vector", lambda e, t=tdb: e.tensor_scalar(
            dball_all[64:128, t * c.P:(t + 1) * c.P], i64_2[64:128, :],
            dbc[64:128, 2 * t + 1:2 * t + 2], None, op0=ALU.mult),
            waits=[dbc_tok], inc=DBA)

    HP2 = 128 // c.P
    HG = c.H // c.G
    zj_free = [None] * 2
    xj_free = [None] * 2
    yg_free = [None, None]
    ygn_free = [None, None]
    pY_free = [None, None]
    scs_free = None
    csrow_free = [None, None]
    csr_tok = [None, None]
    etmp_free = [None, None]
    eb_free = [None, None]
    od_free = [None, None]
    cd_free = [None, None]
    gt_free = [None, None]
    pcs_free = [[(pCS[0], 0)], []]
    pS_free = [None, None]
    bd_free = []
    bdb_rd = [None, None]
    sqb_free = [None, None]
    rs_st_prev = None
    st = {"supd_last": sinit_tok, "sq_prev": None, "rms_mm_prev": None,
          "first_sq": True}
    srun_readers = [sinit_tok]
    r2 = None
    pcs_free[0] = []
    PL = S("plc")
    pending_supd = [None]
    pending_pair = [None]

    def flush_supd():
        if pending_supd[0] is not None:
            fn = pending_supd[0]
            pending_supd[0] = None
            fn()

    def flush_pair():
        if pending_pair[0] is not None:
            fn = pending_pair[0]
            pending_pair[0] = None
            fn()

    for ch in range(c.NCH):
        xts, bts = load_chunk0(ch)
        ctx_m["ptr_free"][0] = scs_free
        ctoks = chunk_trans0(ch, [cs_tok])
        cast_tok = b.emit("vector", lambda e: e.tensor_copy(srun_bf[:], S_f32[:]),
                          waits=[st["supd_last"]] + srun_readers, inc=DV)
        srun_readers = []
        st["first_sq"] = True
        xl = None
        nblk = c.H // CSG

        def csr_load(bi, ch=ch):
            j = bi % 2
            csr_tok[j] = b.emit("scalar", lambda e, bi=bi, j=j, ch=ch: e.dma_start(
                csrow[j][:], s_cs[bi * CSG:(bi + 1) * CSG,
                                  ch * c.C:(ch + 1) * c.C]),
                waits=[cs_st_tok, csrow_free[j]], inc=CSR[j], n=16)

        csr_load(0)

        def front(h):
            ip = h % 2
            bi = h // CSG
            if h % CSG == 0 and bi + 1 < nblk:
                csr_load(bi + 1)
            hh = h % CSG
            cbt = b.emit("tensor", lambda e, hh=hh, ip=ip, j=bi % 2: e.matmul(
                pCS[ip][:, 0:c.C], ones1_f[0:1, :],
                csrow[j][0:1, hh * c.C:(hh + 1) * c.C]),
                waits=pcs_free[ip] + [csr_tok[bi % 2]], inc=TE)
            csrow_free[bi % 2] = cbt
            return cbt

        cb_pipe = {0: front(0)}
        for g in range(c.G):
            gB = g * (c.NST // 128)
            gC = c.NB + gB
            sc_toks = []
            qprev = None
            for sh in range(2):
                q1 = b.emit("tensor", lambda e, gB=gB, gC=gC, sh=sh: e.matmul(
                    pSC[:, 0:c.C],
                    bcb[0][:, gB * c.C + sh * 128:gB * c.C + (sh + 1) * 128],
                    bcb[0][:, gC * c.C:(gC + 1) * c.C]),
                    waits=bts + [scs_free, qprev] + ctoks[-1:], inc=TE)
                q2 = b.emit("scalar", lambda e, sh=sh: e.copy(
                    scsb[:, sh * c.C:(sh + 1) * c.C], pSC[:, 0:c.C]),
                    waits=[q1], inc=AC)
                qprev = q2
                sc_toks.append(q2)
            scs_free = sc_toks[-1]
            trts = []
            for sh in range(2):
                u1 = b.emit("tensor", lambda e, gB=gB, sh=sh: e.transpose(
                    ctx_m["pTRB"][:, 0:128],
                    bcb[0][:, gB * c.C + sh * 128:gB * c.C + (sh + 1) * 128],
                    ident_b[:]), waits=bts + bd_free + [ctx_m["ptrb_free"][0]],
                    inc=TE)
                u2 = b.emit("vector", lambda e, sh=sh: e.tensor_copy(
                    btokb[:, sh * 128:(sh + 1) * 128],
                    ctx_m["pTRB"][0:128, 0:128]), waits=[u1], inc=DV)
                ctx_m["ptrb_free"][0] = u2
                trts.append(u2)
            for hh in range(HG):
                h = g * HG + hh
                hp = h // HP2
                hoff = (h % HP2) * c.P
                iy = hp % 2
                ip = h % 2
                slot = iy
                cb = cb_pipe.pop(h)
                if h + 1 < c.H:
                    cb_pipe[h + 1] = front(h + 1)
                e3 = b.emit("scalar", lambda e, ip=ip: e.activation(
                    odb[ip][:], pCS[ip][:, 0:c.C], AF.Exp),
                    waits=[cb, od_free[ip]], inc=AC)
                d1 = b.emit("vector", lambda e, h=h, ip=ip:
                            e.scalar_tensor_tensor(
                    etmp[ip][:, 0:c.C], pCS[ip][:, 0:c.C], ncstk[:, h:h + 1],
                    mng[:, 0:c.C], op0=ALU.add, op1=ALU.min),
                    waits=[cb, etmp_free[ip], mng_tok] + ctoks, inc=DV)
                d2 = b.emit("vector", lambda e, h=h, ip=ip:
                            e.scalar_tensor_tensor(
                    etmp[ip][:, c.C:2 * c.C], pCS[ip][:, 0:c.C],
                    ncstk[:, c.H + h:c.H + h + 1],
                    mng[:, c.C:2 * c.C], op0=ALU.add, op1=ALU.min),
                    waits=[cb], inc=DV)
                e1 = b.emit("scalar", lambda e, ip=ip: e.activation(
                    ebuf[ip][:, 0:c.C], etmp[ip][:, 0:c.C], AF.Exp),
                    waits=[d1, eb_free[ip]], inc=AC)
                e2 = b.emit("scalar", lambda e, ip=ip: e.activation(
                    ebuf[ip][:, c.C:2 * c.C], etmp[ip][:, c.C:2 * c.C], AF.Exp),
                    waits=[d2], inc=AC)
                pcs_free[ip] = [d1, d2, e3]
                etmp_free[ip] = e2
                g1ts = []
                for sh in range(2):
                    g1t = b.emit("scalar", lambda e, sh=sh, h=h, ip=ip: e.mul(
                        gtb[ip][:, sh * c.C:(sh + 1) * c.C],
                        scsb[:, sh * c.C:(sh + 1) * c.C],
                        dttk[:, sh * c.H + h:sh * c.H + h + 1]),
                        waits=sc_toks + ctoks + [gt_free[ip]], inc=AC)
                    g1ts.append(g1t)
                v1 = b.emit("scalar", lambda e, h=h, ip=ip: e.mul(
                    bdb[:, ip * 256:ip * 256 + 128], btokb[:, 0:128],
                    ddmtk[:, h:h + 1]), waits=trts + ctoks + [bdb_rd[ip]],
                    inc=AC)
                v2 = b.emit("scalar", lambda e, h=h, ip=ip: e.mul(
                    bdb[:, ip * 256 + 128:ip * 256 + 256], btokb[:, 128:256],
                    ddmtk[:, c.H + h:c.H + h + 1]),
                    waits=trts + ctoks + [bdb_rd[ip]], inc=AC)
                c1 = b.emit("gpsimd", lambda e, gC=gC, ip=ip: e.tensor_mul(
                    cdb[ip][:], bcb[0][:, gC * c.C:(gC + 1) * c.C], odb[ip][:]),
                    waits=[e3, cd_free[ip]] + bts, inc=PL)
                od_free[ip] = c1
                g_toks = []
                for sh in range(2):
                    g2t = b.emit("vector", lambda e, sh=sh, ip=ip: e.tensor_mul(
                        gtb[ip][:, sh * c.C:(sh + 1) * c.C],
                        gtb[ip][:, sh * c.C:(sh + 1) * c.C],
                        ebuf[ip][:, sh * c.C:(sh + 1) * c.C]),
                        waits=[g1ts[sh], e1 if sh == 0 else e2], inc=DV)
                    g_toks.append(g2t)
                gt_free[ip] = g_toks[-1]
                eb_free[ip] = g_toks[1]
                m1 = b.emit("tensor", lambda e, h=h, ip=ip: e.matmul(
                    pS[ip][:, 0:c.P], bdb[:, ip * 256:ip * 256 + 128],
                    xtk[0][0][:, h * c.P:(h + 1) * c.P], start=True, stop=False),
                    waits=[v1] + xts + [pS_free[ip]], inc=None)
                m2 = b.emit("tensor", lambda e, h=h, ip=ip: e.matmul(
                    pS[ip][:, 0:c.P], bdb[:, ip * 256 + 128:ip * 256 + 256],
                    xtk[0][1][:, h * c.P:(h + 1) * c.P], start=False, stop=True),
                    waits=[v2], inc=TE)
                bd_free = [m2]
                bdb_rd[ip] = m2
                if hoff == 0:
                    xl = b.emit("scalar", lambda e, ch=ch, hp=hp, slot=slot: e.dma_start(
                        xjit[slot][:],
                        s_xh[hp * 128:(hp + 1) * 128, ch * c.C:(ch + 1) * c.C]),
                        waits=[xj_free[slot]], inc=XJ[slot], n=16)
                y1 = b.emit("tensor", lambda e, h=h, iy=iy, hoff=hoff, ip=ip: e.matmul(
                    pY[iy][hoff:hoff + c.P, 0:c.C],
                    xtk[0][0][:, h * c.P:(h + 1) * c.P], gtb[ip][:, 0:c.C],
                    start=True, stop=False),
                    waits=[g_toks[0]] + xts + [pY_free[iy]], inc=None)
                b.emit("tensor", lambda e, h=h, iy=iy, hoff=hoff, ip=ip: e.matmul(
                    pY[iy][hoff:hoff + c.P, 0:c.C],
                    xtk[0][1][:, h * c.P:(h + 1) * c.P], gtb[ip][:, c.C:2 * c.C],
                    start=False, stop=False), waits=[g_toks[1]], inc=None)
                b.emit("tensor", lambda e, h=h, iy=iy, hoff=hoff, ip=ip: e.matmul(
                    pY[iy][hoff:hoff + c.P, 0:c.C],
                    srun_bf[:, h * c.P:(h + 1) * c.P], cdb[ip][:],
                    start=False, stop=False), waits=[c1, cast_tok], inc=None)
                y4 = b.emit("tensor", lambda e, h=h, iy=iy, hoff=hoff, slot=slot:
                            e.matmul(
                    pY[iy][hoff:hoff + c.P, 0:c.C],
                    dball_all[hoff:hoff + 64, (h // 2) * c.P:(h // 2) * c.P + 64],
                    xjit[slot][hoff:hoff + c.P, :],
                    start=False, stop=True),
                    waits=[dba_tok, xl],
                    inc=(TE if hoff + c.P == 128 else None))
                xj_free[slot] = y4

                supd = b.emit("vector", lambda e, h=h, ch=ch, ip=ip:
                              e.scalar_tensor_tensor(
                    S_f32[:, h * c.P:(h + 1) * c.P],
                    S_f32[:, h * c.P:(h + 1) * c.P],
                    e_all[:, ch * c.H + h:ch * c.H + h + 1],
                    pS[ip][:, 0:c.P], op0=ALU.mult, op1=ALU.add),
                    waits=[m2, cast_tok, ea_tok], inc=DV)
                pS_free[ip] = supd
                st["supd_last"] = supd
                if hoff + c.P == 128:
                    flush_pair()

                    def mk_pair(iy=iy, slot=slot, ch=ch, hp=hp, ip=ip, y4=y4):
                        def emitfn():
                            zl = b.emit("scalar", lambda e: e.dma_start(
                                zjit[slot][:],
                                s_z[hp * 128:(hp + 1) * 128,
                                    ch * c.C:(ch + 1) * c.C]),
                                waits=[zj_free[slot]], inc=ZJ[slot], n=16)
                            v1y = b.emit("vector", lambda e: e.tensor_mul(
                                ygb[iy][:], pY[iy][:, 0:c.C], zjit[slot][:]),
                                waits=[y4, zl, yg_free[iy]], inc=DV)
                            pY_free[iy] = v1y
                            zj_free[slot] = v1y
                            s1 = b.emit("scalar", lambda e: e.activation(
                                sqb[ip][:], ygb[iy][:], AF.Square),
                                waits=[v1y, sqb_free[ip]], inc=AC)
                            if st["first_sq"]:
                                s2 = b.emit("gpsimd", lambda e: e.tensor_copy(
                                    rmsacc[:], sqb[ip][:]),
                                    waits=[s1, st["rms_mm_prev"]], inc=PL)
                                st["first_sq"] = False
                            else:
                                s2 = b.emit("gpsimd", lambda e: e.tensor_add(
                                    rmsacc[:], rmsacc[:], sqb[ip][:]),
                                    waits=[s1, st["sq_prev"]], inc=PL)
                            sqb_free[ip] = s2
                            st["sq_prev"] = s2
                            v2y = b.emit("scalar", lambda e: e.mul(
                                ygnb[iy][:], ygb[iy][:], nw_s[:, hp:hp + 1]),
                                waits=[v1y, ygn_free[iy]], inc=AC)
                            yg_free[iy] = v2y
                            ygn_free[iy] = b.emit("gpsimd", lambda e: e.dma_start(
                                s_ygn[hp, :, ch * c.C:(ch + 1) * c.C], ygnb[iy][:]),
                                waits=[v2y], inc=YGST[iy], n=16)
                        return emitfn
                    pending_pair[0] = mk_pair()
                    flush_pair()
                g1_goal = ((ch * c.H + h + 1) * len(g1_units)) // (c.NCH * c.H)
                g1_run(g1_goal - g1_pos[0])
        flush_pair()
        srun_readers = [st["supd_last"]] + bd_free
        r1 = b.emit("tensor", lambda e: e.matmul(
            pSC[0:1, 0:c.C], ones128_f[:], rmsacc[:]),
            waits=[st["sq_prev"], scs_free], inc=TE)
        r2a = b.emit("scalar", lambda e, ch=ch: e.activation(
            sqb[0][0:1, 0:c.C], pSC[0:1, 0:c.C], AF.Sqrt,
            bias=float(c.eps_rms), scale=1.0 / c.DI),
            waits=[r1, sqb_free[0], sqb_free[1]], inc=AC)
        r2v = b.emit("vector", lambda e, ch=ch: e.reciprocal(
            rs_piece[0:1, 0:c.C], sqb[0][0:1, 0:c.C]),
            waits=[r2a, rs_st_prev], inc=DV)
        r2 = b.emit("sync", lambda e, ch=ch: e.dma_start(
            s_rs[0:1, ch * c.C:(ch + 1) * c.C], rs_piece[0:1, 0:c.C]),
            waits=[r2v], inc=RSST, n=16)
        rs_st_prev = r2
        st["rms_mm_prev"] = r1
        scs_free = r2a
        sqb_free[0] = r2v
        for sh in range(2):
            xtk_free[sh] = bd_free[0]
        bcb_free[0] = bd_free[0]
    g1_run(len(g1_units))
    rs_tok = r2
    b.barrier()
    b.flush()
    esM.close()
    es_ssd.close()
    esG1.close()

    es_xn2 = contextlib.ExitStack()
    xn_fm = sb(es_xn2, "xn_fm2", [128, c.KD * c.T], BF16, side="right")
    XT2 = S("xntr2")
    for k in range(c.KD):
        b.emit("sync", lambda e, k=k: e.dma_start(
            xn_fm[:, k * c.T:(k + 1) * c.T],
            s_xn[:, k * 128:(k + 1) * 128], transpose=True), inc=XT2, n=16)
    b.barrier()
    b.flush()

    wgb = [sb(esG2, f"wgb{i}", [128, c.KD * 128], BF16) for i in range(2)]
    wub = [sb(esG2, f"wub{i}", [128, c.KD * 128], BF16) for i in range(2)]
    znb = [sb(esG2, f"znb{i}", [c.G_GU, 512], BF16) for i in range(2)]
    ZNL = [S("znl0"), S("znl1")]
    hbuf = sb(esG2, "hbuf", [128, c.T // 2], BF16)
    sgate = sb(esG2, "sgate", [128, c.T // 2], BF16)
    TH = c.T // 2
    pGU = [pt(esG2, f"pGU{i}", [128, TH]) for i in range(4)]
    WGL = [S("wgl0"), S("wgl1")]
    WUL = [S("wul0"), S("wul1")]
    HST = S("hst")
    wg_free = [None, None]
    wu_free = [None, None]
    pgu_free = [None, None, None, None]
    hb_free = None
    sg_free = None

    def wgu_src(side, mp):
        col = side * c.INTER + mp * 128
        return s_wgu[:, :, col:col + 128].transpose([1, 0, 2])

    zn_free = [None, None]
    for mp in range(c.KF):
        i = mp % 2
        lg = b.emit("sync", lambda e, i=i, mp=mp: e.dma_start(
            wgb[i][:].rearrange("p (k j) -> p k j", j=128), wgu_src(0, mp)),
            waits=[wg_free[i]], inc=WGL[i], n=16)
        lu = b.emit("sync", lambda e, i=i, mp=mp: e.dma_start(
            wub[i][:].rearrange("p (k j) -> p k j", j=128), wgu_src(1, mp)),
            waits=[wu_free[i]], inc=WUL[i], n=16)
        lz = None
        for qi, (hl, side) in enumerate(((0, 0), (0, 1), (1, 0), (1, 1))):
            lz = b.emit("sync", lambda e, i=i, mp=mp, qi=qi, hl=hl, side=side:
                        e.dma_start(
                znb[i][:, qi * 128:(qi + 1) * 128],
                s_zgu[hl, :, side * c.INTER + mp * 128:
                      side * c.INTER + (mp + 1) * 128]),
                waits=[zn_free[i], zn_tok] if qi == 0 else [], inc=ZNL[i], n=16)
        h_toks = []
        for th in range(2):
            pg, pu = pGU[2 * (th % 2)], pGU[2 * (th % 2) + 1]
            nbs = [(nb * 512, min(512, TH - nb * 512))
                   for nb in range((TH + 511) // 512)]
            mg = mu = None
            for k in range(c.KD):
                for nb, (no, nw) in enumerate(nbs):
                    co = th * TH + no
                    mg = b.emit("tensor", lambda e, i=i, k=k, co=co, no=no, nw=nw,
                                pg=pg: e.matmul(
                        pg[:, no:no + nw],
                        wgb[i][:, k * 128:(k + 1) * 128],
                        xn_fm[:, k * c.T + co:k * c.T + co + nw],
                        start=(k == 0), stop=False),
                        waits=([lg, pgu_free[2 * (th % 2)]] if k == 0 and nb == 0 else []),
                        inc=None)
                    mu = b.emit("tensor", lambda e, i=i, k=k, co=co, no=no, nw=nw,
                                pu=pu: e.matmul(
                        pu[:, no:no + nw],
                        wub[i][:, k * 128:(k + 1) * 128],
                        xn_fm[:, k * c.T + co:k * c.T + co + nw],
                        start=(k == 0), stop=False),
                        waits=([lu, pgu_free[2 * (th % 2) + 1]] if k == 0 and nb == 0 else []),
                        inc=None)
            for (xg_, zi, lastf) in ((xg_hi, 0, False), (xg_hi, 1, False),
                                     (xg_lo, 0, True)):
                for nb, (no, nw) in enumerate(nbs):
                    co = th * TH + no
                    mg = b.emit("tensor", lambda e, no=no, nw=nw, co=co, xg_=xg_,
                                zi=zi, i=i, pg=pg, lastf=lastf: e.matmul(
                        pg[:, no:no + nw],
                        znb[i][:, zi * 256:zi * 256 + 128], xg_[:, co:co + nw],
                        start=False, stop=lastf),
                        waits=[xg_tok, lz],
                        inc=(TE if lastf and nb == len(nbs) - 1 else None))
                    mu = b.emit("tensor", lambda e, no=no, nw=nw, co=co, xg_=xg_,
                                zi=zi, i=i, pu=pu, lastf=lastf: e.matmul(
                        pu[:, no:no + nw],
                        znb[i][:, zi * 256 + 128:zi * 256 + 256],
                        xg_[:, co:co + nw], start=False, stop=lastf),
                        waits=[],
                        inc=(TE if lastf and nb == len(nbs) - 1 else None))
            wg_free[i] = mg
            wu_free[i] = mu
            zn_free[i] = mu
            sg = b.emit("scalar", lambda e, pg=pg: e.activation(
                sgate[:], pg[:], AF.Silu), waits=[mg, sg_free], inc=AC)
            pgu_free[2 * (th % 2)] = sg
            hh2 = b.emit("vector", lambda e, pu=pu: e.tensor_mul(
                hbuf[:], sgate[:], pu[:]),
                waits=[sg, mu, hb_free], inc=DV)
            pgu_free[2 * (th % 2) + 1] = hh2
            sg_free = hh2
            hb_free = b.emit("sync", lambda e, mp=mp, th=th: e.dma_start(
                s_h[mp, :, th * TH:(th + 1) * TH], hbuf[:]),
                waits=[hh2], inc=HST, n=16)
            h_toks.append(hb_free)
    b.barrier()
    b.flush()
    esG2.close()
    es_xn2.close()
    es_mid.close()
    esO = contextlib.ExitStack()
    KBLK = 8
    nkb = (c.KDI + KBLK - 1) // KBLK
    ygn_blk = sb(esO, "ygn_blk", [128, KBLK * c.T], BF16)
    wo_blk = sb(esO, "wo_blk", [128, KBLK * c.D], BF16)
    evo = [sb(esO, f"evo{i}", [128, 2048], F32) for i in range(2)]
    pO = [pt(esO, f"pO{i}", [128, 2048]) for i in range(2)]
    YGL, WOL = S("ygl"), S("wol")
    UAC = [S("uac0"), S("uac1")]
    DH = (c.D + 2047) // 2048
    evo_free = [None, None]
    po_free = [None, None]
    blk_free = None
    for kb in range(nkb):
        nkk = min(KBLK, c.KDI - kb * KBLK)
        b.wait("gpsimd", (UAC[0], UAC[0].count))
        b.wait("gpsimd", (UAC[1], UAC[1].count))
        la = lb = None
        for kk in range(nkk):
            la = b.emit("sync", lambda e, kb=kb, kk=kk: e.dma_start(
                ygn_blk[:, kk * c.T:(kk + 1) * c.T], s_ygn[kb * KBLK + kk, :, :]),
                waits=[blk_free] if kk == 0 else [], inc=YGL, n=16)
            lb = b.emit("gpsimd", lambda e, kb=kb, kk=kk: e.dma_start(
                wo_blk[:, kk * c.D:(kk + 1) * c.D],
                woutT_d[(kb * KBLK + kk) * 128:(kb * KBLK + kk + 1) * 128, :]),
                waits=[blk_free] if kk == 0 else [], inc=WOL, n=16)
        lts = [la, lb]
        last_mm = None
        for tt in range(c.TT):
            for dh in range(DH):
                io = (tt * DH + dh) % 2
                dw = min(2048, c.D - dh * 2048)
                mm = None
                for kk in range(nkk):
                    for nb2 in range(dw // 512):
                        mm = b.emit("tensor", lambda e, kk=kk, tt=tt, dh=dh,
                                    nb2=nb2, io=io: e.matmul(
                            pO[io][:, nb2 * 512:(nb2 + 1) * 512],
                            ygn_blk[:, kk * c.T + tt * 128:kk * c.T + (tt + 1) * 128],
                            wo_blk[:, kk * c.D + dh * 2048 + nb2 * 512:
                                   kk * c.D + dh * 2048 + (nb2 + 1) * 512],
                            start=(kk == 0), stop=(kk == nkk - 1)),
                            waits=(lts + [po_free[io]]) if kk == 0 and nb2 == 0 else [],
                            inc=(TE if kk == nkk - 1 and nb2 == dw // 512 - 1 else None))
                ev = b.emit("scalar", lambda e, io=io, dw=dw: e.copy(
                    evo[io][:, 0:dw], pO[io][:, 0:dw]),
                    waits=[mm, evo_free[io]], inc=AC)
                po_free[io] = ev
                evo_free[io] = b.emit("gpsimd", lambda e, io=io, tt=tt, dh=dh,
                                      dw=dw, kb=kb: e.dma_start(
                    s_u[tt * 128:(tt + 1) * 128, dh * 2048:dh * 2048 + dw],
                    evo[io][:, 0:dw],
                    accum_op=(ALU.bypass if kb == 0 else ALU.add)),
                    waits=[ev], inc=UAC[io], n=16)
                last_mm = mm
        blk_free = last_mm
    b.barrier()
    b.flush()
    esO.close()

    esD = contextlib.ExitStack()
    FBLK = 8
    nfb = (c.KF + FBLK - 1) // FBLK
    qdb = sb(esD, "qdb", [128, c.D], U32)
    nibd = sb(esD, "nibd", [128, c.D], BF16)
    scdb = sb(esD, "scdb", [128, c.D], F32)
    scdbf = sb(esD, "scdbf", [128, c.D], BF16)
    ohd = sb(esD, "ohd", [128, c.G_D], BF16)
    gdf = sb(esD, "gdf", [128, c.KF], F32)
    xgd_hi = sb(esD, "xgd_hi", [c.G_D, c.T], BF16)
    xgd_lo = sb(esD, "xgd_lo", [c.G_D, c.T], BF16)
    xgd_f = sb(esD, "xgd_f", [c.G_D, c.T], F32)
    znd_hi = sb(esD, "znd_hi", [c.G_D, c.D], BF16)
    znd_lo = sb(esD, "znd_lo", [c.G_D, c.D], BF16)
    HL, QDL, SDL, ZDL = S("hl"), S("qdl"), S("sdl"), S("zdl")
    DAC = [S("dac0"), S("dac1")]
    esDz = contextlib.ExitStack()
    DZSL = min(2048, c.D)
    zqd_s = sb(esDz, "zqds", [c.G_D, c.D // 8], U32)
    zfd = sb(esDz, "zfd", [c.G_D, DZSL], F32)
    ztud = sb(esDz, "ztud", [c.G_D, DZSL // 8], U32)
    ztfd = sb(esDz, "ztfd", [c.G_D, DZSL // 8], F32)
    sczd = sb(esDz, "sczd", [c.G_D, DZSL], F32)
    ZQDL = S("zqdl")
    zqt = b.emit("sync", lambda e: e.dma_start(zqd_s[:], qz_d_d[:]), inc=ZQDL, n=16)
    znd_tok = None
    prevs = []
    o = 0
    while o < c.D:
        w = min(DZSL, c.D - o)
        lsc = b.emit("sync", lambda e, o=o, w=w: e.dma_start(
            sczd[:, 0:w], sc_d_d[:, o:o + w]), waits=prevs, inc=ZDL, n=16)
        ztoks = []
        for sft in range(8):
            u1 = b.emit("vector", lambda e, o=o, w=w, sft=sft: e.tensor_scalar(
                ztud[:, 0:w // 8], zqd_s[:, o // 8:(o + w) // 8],
                (15 << (4 * sft)), None, op0=ALU.bitwise_and),
                waits=[zqt] + ztoks[-3:] + prevs, inc=DV)
            u2 = b.emit("vector", lambda e, w=w: e.tensor_copy(
                ztfd[:, 0:w // 8], ztud[:, 0:w // 8]), waits=[u1], inc=DV)
            u3 = b.emit("vector", lambda e, w=w, sft=sft: e.tensor_scalar_mul(
                zfd[:, sft:w:8], ztfd[:, 0:w // 8],
                float(2.0 ** (-4 * sft))), waits=[u2], inc=DV)
            ztoks += [u1, u2, u3]
        t = b.emit("vector", lambda e, w=w: e.tensor_scalar_add(
            zfd[:, 0:w], zfd[:, 0:w], 1.0), waits=ztoks, inc=DV)
        t = b.emit("vector", lambda e, w=w: e.tensor_mul(
            zfd[:, 0:w], zfd[:, 0:w], sczd[:, 0:w]), waits=[t, lsc], inc=DV)
        t = b.emit("vector", lambda e, w=w: e.tensor_scalar_mul(
            zfd[:, 0:w], zfd[:, 0:w], -1.0), waits=[t], inc=DV)
        th = b.emit("vector", lambda e, o=o, w=w: e.tensor_copy(
            znd_hi[:, o:o + w], zfd[:, 0:w]), waits=[t], inc=DV)
        t2 = b.emit("vector", lambda e, o=o, w=w: e.tensor_sub(
            sczd[:, 0:w], zfd[:, 0:w], znd_hi[:, o:o + w]), waits=[th], inc=DV)
        tl = b.emit("vector", lambda e, o=o, w=w: e.tensor_copy(
            znd_lo[:, o:o + w], sczd[:, 0:w]), waits=[t2], inc=DV)
        znd_tok = tl
        prevs = [t2, tl]
        o += w
    b.barrier()
    b.flush()
    esDz.close()
    h_blk = sb(esD, "h_blk", [128, FBLK * c.T], BF16)
    wd_blk = sb(esD, "wd_blk", [128, FBLK * c.D], BF16)
    evd = [sb(esD, f"evd{i}", [128, 2048], F32) for i in range(2)]
    pD = [pt(esD, f"pD{i}", [128, 2048]) for i in range(2)]
    tg = b.emit("vector", lambda e: e.tensor_copy(gdf[:], gidx_d_s[:]), inc=DV)
    ev_free = [None, None]
    pd_free = [None, None]
    hblk_free = None
    wd_free = None
    nibd_free = None
    scd_free = None
    oh_free = None
    for fb in range(nfb):
        nkk = min(FBLK, c.KF - fb * FBLK)
        b.wait("gpsimd", (DAC[0], DAC[0].count))
        b.wait("gpsimd", (DAC[1], DAC[1].count))
        lh_last = None
        dq = []
        for kk in range(nkk):
            kf = fb * FBLK + kk
            lh_last = b.emit("sync", lambda e, kf=kf, kk=kk: e.dma_start(
                h_blk[:, kk * c.T:(kk + 1) * c.T], s_h[kf, :, :]),
                waits=[hblk_free] if kk == 0 else [], inc=HL, n=16)
            q1 = b.emit("gpsimd", lambda e, kf=kf: e.dma_start(
                qdb[:], qw_d_d[kf, :, :]), waits=[nibd_free], inc=QDL, n=16)
            q2 = b.emit("gpsimd", lambda e, kf=kf: e.indirect_dma_start(
                scdb[:], None, sc_d_d[:, :], IOA(gidx_d_s[:, kf:kf + 1], 0)),
                waits=[scd_free], inc=SDL, n=16)
            d1 = b.emit("vector", lambda e: e.tensor_scalar(
                qdb[:], qdb[:], m15[:], None, op0=ALU.bitwise_and),
                waits=[q1], inc=DV)
            d2 = b.emit("vector", lambda e: e.tensor_copy(nibd[:], qdb[:]),
                        waits=[d1, nibd_free], inc=DV)
            nibd_free = d2
            d3 = b.emit("vector", lambda e: e.tensor_scalar_mul(
                nibd[:], nibd[:], minv[:]), waits=[d2], inc=DV)
            d4 = b.emit("vector", lambda e: e.tensor_copy(scdbf[:], scdb[:]),
                        waits=[q2, scd_free], inc=DV)
            scd_free = d4
            d5 = b.emit("vector", lambda e, kk=kk: e.tensor_mul(
                wd_blk[:, kk * c.D:(kk + 1) * c.D], nibd[:], scdbf[:]),
                waits=([d3, d4, wd_free] if kk == 0 else [d3, d4]), inc=DV)
            nibd_free = d5
            scd_free = d5
            dq.append(d5)
        mmx = None
        for kk in range(nkk):
            kf = fb * FBLK + kk
            otk = b.emit("vector", lambda e, kf=kf: e.tensor_scalar(
                ohd[:], iota_d[:], gdf[:, kf:kf + 1], None, op0=ALU.is_equal),
                waits=[tg, oh_free], inc=DV)
            for nb2 in range(c.T // 512):
                mmx = b.emit("tensor", lambda e, kk=kk, nb2=nb2: e.matmul(
                    pD[0][0:c.G_D, nb2 * 512:(nb2 + 1) * 512], ohd[:],
                    h_blk[:, kk * c.T + nb2 * 512:kk * c.T + (nb2 + 1) * 512],
                    start=(kk == 0), stop=(kk == nkk - 1)),
                    waits=([otk, lh_last, pd_free[0]]) if kk == 0 and nb2 == 0 else [otk],
                    inc=TE)
            oh_free = mmx
        xt1 = b.emit("vector", lambda e: e.tensor_copy(
            xgd_f[:], pD[0][0:c.G_D, 0:c.T]), waits=[mmx], inc=DV)
        xt2 = b.emit("vector", lambda e: e.tensor_copy(xgd_hi[:], xgd_f[:]),
                     waits=[xt1], inc=DV)
        xt3 = b.emit("vector", lambda e: e.tensor_sub(xgd_f[:], xgd_f[:], xgd_hi[:]),
                     waits=[xt2], inc=DV)
        xgd_tok = b.emit("vector", lambda e: e.tensor_copy(xgd_lo[:], xgd_f[:]),
                         waits=[xt3], inc=DV)
        pd_free[0] = xgd_tok
        last_mm = None
        for tt in range(c.TT):
            for dh in range(DH):
                io = (tt * DH + dh) % 2
                dw = min(2048, c.D - dh * 2048)
                mm = None
                for kk in range(nkk):
                    for nb2 in range(dw // 512):
                        mm = b.emit("tensor", lambda e, kk=kk, tt=tt, dh=dh,
                                    nb2=nb2, io=io: e.matmul(
                            pD[io][:, nb2 * 512:(nb2 + 1) * 512],
                            h_blk[:, kk * c.T + tt * 128:kk * c.T + (tt + 1) * 128],
                            wd_blk[:, kk * c.D + dh * 2048 + nb2 * 512:
                                   kk * c.D + dh * 2048 + (nb2 + 1) * 512],
                            start=(kk == 0), stop=False),
                            waits=(dq + [pd_free[io], xgd_tok, lh_last]) if kk == 0 and nb2 == 0 else [],
                            inc=None)
                for ci2, (xg_, zn_, lastf) in enumerate(
                        ((xgd_hi, znd_hi, False), (xgd_hi, znd_lo, False),
                         (xgd_lo, znd_hi, True))):
                    for nb2 in range(dw // 512):
                        mm = b.emit("tensor", lambda e, xg_=xg_, zn_=zn_, tt=tt,
                                    dh=dh, nb2=nb2, io=io, lastf=lastf: e.matmul(
                            pD[io][:, nb2 * 512:(nb2 + 1) * 512],
                            xg_[:, tt * 128:(tt + 1) * 128],
                            zn_[:, dh * 2048 + nb2 * 512:dh * 2048 + (nb2 + 1) * 512],
                            start=False, stop=lastf), waits=[znd_tok],
                            inc=(TE if lastf and nb2 == dw // 512 - 1 else None))
                ev = b.emit("scalar", lambda e, io=io, dw=dw: e.copy(
                    evd[io][:, 0:dw], pD[io][:, 0:dw]),
                    waits=[mm, ev_free[io]], inc=AC)
                pd_free[io] = ev
                ev_free[io] = b.emit("gpsimd", lambda e, io=io, tt=tt, dh=dh,
                                     dw=dw, fb=fb: e.dma_start(
                    s_dn[tt * 128:(tt + 1) * 128, dh * 2048:dh * 2048 + dw],
                    evd[io][:, 0:dw],
                    accum_op=(ALU.bypass if fb == 0 else ALU.add)),
                    waits=[ev], inc=DAC[io], n=16)
                last_mm = mm
        hblk_free = last_mm
        wd_free = last_mm
    b.barrier()
    b.flush()
    esD.close()

    esC = contextlib.ExitStack()
    xc2 = [sb(esC, f"xc2{i}", [128, c.D], F32) for i in range(2)]
    xnc = [sb(esC, f"xnc{i}", [128, c.D], BF16) for i in range(2)]
    uc = [sb(esC, f"uc{i}", [128, c.D], F32) for i in range(2)]
    dnc = [sb(esC, f"dnc{i}", [128, c.D], F32) for i in range(2)]
    gsil = sb(esC, "gsil", [128, c.D], F32)
    sbb = sb(esC, "sbb", [128, c.D], F32)
    ob = [sb(esC, f"ob{i}", [128, c.D], F32) for i in range(2)]
    rs_col = sb(esC, "rs_col", [128, c.TT], F32)
    rs_sb = sb(esC, "rs_sb", [1, 128], F32)
    RSL = S("rsl")
    pC2 = pt(esC, "pC2", [128, 512])
    CLX = [S("clx0"), S("clx1")]
    CLN = [S("cln0"), S("cln1")]
    CLU = [S("clu0"), S("clu1")]
    CLD = [S("cld0"), S("cld1")]
    OST = [S("ost0"), S("ost1")]
    rstoks = []
    rprev = None
    rld_prev = None
    for tt in range(c.TT):
        rld = b.emit("sync", lambda e, tt=tt: e.dma_start(
            rs_sb[0:1, 0:128], s_rs[0:1, tt * 128:(tt + 1) * 128]),
            waits=[rs_tok, rld_prev], inc=RSL, n=16)
        rt = b.emit("tensor", lambda e, tt=tt: e.transpose(
            pC2[0:128, 0:1], rs_sb[0:1, 0:128], ident_f[0:1, 0:1]),
            waits=[rprev, rld], inc=TE)
        rld_prev = rt
        rprev = b.emit("vector", lambda e, tt=tt: e.tensor_copy(
            rs_col[:, tt:tt + 1], pC2[0:128, 0:1]), waits=[rt], inc=DV)
        rstoks.append(rprev)
    rsc_tok = rstoks[-1]
    x_free = [None, None]
    xn_free = [None, None]
    u_free = [None, None]
    dn_free = [None, None]
    ob_free = [None, None]
    gs_free = None
    sb_free = None
    for tt in range(c.TT):
        i = tt % 2
        l1 = b.emit("sync", lambda e, i=i, tt=tt: e.dma_start(
            xc2[i][:], x_d[tt * 128:(tt + 1) * 128, :]), waits=[x_free[i]],
            inc=CLX[i], n=16)
        l2 = b.emit("sync", lambda e, i=i, tt=tt: e.dma_start(
            xnc[i][:], s_xn[tt * 128:(tt + 1) * 128, :]), waits=[xn_free[i]],
            inc=CLN[i], n=16)
        l3 = b.emit("sync", lambda e, i=i, tt=tt: e.dma_start(
            uc[i][:], s_u[tt * 128:(tt + 1) * 128, :]), waits=[u_free[i]],
            inc=CLU[i], n=16)
        l4 = b.emit("sync", lambda e, i=i, tt=tt: e.dma_start(
            dnc[i][:], s_dn[tt * 128:(tt + 1) * 128, :]), waits=[dn_free[i]],
            inc=CLD[i], n=16)
        g1 = b.emit("scalar", lambda e, i=i: e.activation(
            gsil[:], dnc[i][:], AF.Silu), waits=[l4, gs_free], inc=AC)
        dn_free[i] = g1
        s1 = b.emit("vector", lambda e, i=i, tt=tt: e.scalar_tensor_tensor(
            sbb[:], uc[i][:], rs_col[:, tt:tt + 1], xnc[i][:],
            op0=ALU.mult, op1=ALU.add), waits=[l3, l2, rsc_tok, sb_free], inc=DV)
        u_free[i] = s1
        xn_free[i] = s1
        s2 = b.emit("vector", lambda e: e.tensor_mul(sbb[:], sbb[:], gsil[:]),
                    waits=[s1, g1], inc=DV)
        gs_free = s2
        o1 = b.emit("vector", lambda e, i=i: e.tensor_add(
            ob[i][:], sbb[:], xc2[i][:]), waits=[s2, l1, ob_free[i]], inc=DV)
        sb_free = o1
        x_free[i] = o1
        ob_free[i] = b.emit("sync", lambda e, i=i, tt=tt: e.dma_start(
            out_d[tt * 128:(tt + 1) * 128, :], ob[i][:]), waits=[o1], inc=OST[i], n=16)
    b.wait("gpsimd", (OST[0], OST[0].count))
    b.wait("gpsimd", (OST[1], OST[1].count))
    b.flush()
    esC.close()
    return nc


def kernel(**inputs):
    cfg = CFG_REAL
    in_maps = make_in_maps(cfg, inputs)
    nc = build(cfg)
    res = run_bass_kernel_spmd(nc, in_maps, core_ids=list(range(cfg.n_cores)))
    x = np.asarray(inputs["L0_attn_hidden_states"])
    out = np.empty((cfg.batch, 2 * cfg.T, cfg.D), dtype=np.float32)
    for core in range(cfg.n_cores):
        bb, half = core // 2, core % 2
        out[bb, half * cfg.T:(half + 1) * cfg.T, :] = res.results[core]["out"]
    return out.reshape(x.shape)


# version 12
# speedup vs baseline: 1.0195x; 1.0195x over previous
import contextlib
import numpy as np
import ml_dtypes
import concourse.bass as bass
import concourse.mybir as mybir
from concourse.bass_utils import run_bass_kernel_spmd

F32 = mybir.dt.float32
BF16 = mybir.dt.bfloat16
U32 = mybir.dt.uint32
I32 = mybir.dt.int32
AF = mybir.ActivationFunctionType
ALU = mybir.AluOpType
AX = mybir.AxisListType
BF = ml_dtypes.bfloat16
IOA = bass.IndirectOffsetOnAxis

ENGS = ("tensor", "vector", "scalar", "sync", "gpsimd")


class Cfg:
    def __init__(self, D=4096, DI=8192, H=128, G=8, NST=128, CK=4, C=256,
                 T=2048, INTER=11008, batch=4, n_cores=8,
                 eps_ln=1e-5, eps_rms=1e-5):
        self.D, self.DI, self.H, self.G, self.NST, self.CK, self.C, self.T = \
            D, DI, H, G, NST, CK, C, T
        self.INTER, self.batch, self.n_cores = INTER, batch, n_cores
        self.eps_ln, self.eps_rms = eps_ln, eps_rms
        self.P = DI // H
        self.GN = G * NST
        self.conv_dim = DI + 2 * self.GN
        self.d_in = 2 * DI + 2 * self.GN + H
        self.KD = D // 128
        self.KDI = DI // 128
        self.KF = INTER // 128
        self.NB = self.GN // 128
        self.NXBC = self.conv_dim // 128
        self.NCH = T // C
        self.M_IN = (self.d_in + 127) // 128
        self.JTOT = 2 * INTER
        self.G_GU = D // 128
        self.G_D = INTER // 128
        self.TT = T // 128
        assert D % 128 == 0 and DI % 128 == 0 and self.GN % 128 == 0
        assert INTER % 128 == 0 and T % C == 0 and C == 256
        assert H <= 128 and self.P == 64 and NST == 128 and self.H % self.G == 0


CFG_REAL = Cfg()


class Sem:
    __slots__ = ("h", "count", "name")


class B:

    def __init__(self, nc, es):
        self.nc, self.es = nc, es
        self.sems = {}
        self.seen = {e: {} for e in ENGS}
        self.thunks = {e: [] for e in ENGS}

    def sem(self, name):
        if name not in self.sems:
            s = Sem()
            s.h = self.es.enter_context(self.nc.semaphore(name))
            s.count = 0
            s.name = name
            self.sems[name] = s
        return self.sems[name]

    def wait(self, eng, tok):
        if tok is None:
            return
        sem, v = tok
        if v <= 0 or self.seen[eng].get(sem.name, 0) >= v:
            return
        self.seen[eng][sem.name] = v
        self.thunks[eng].append(lambda e, s=sem.h, vv=v: e.wait_ge(s, vv))

    def emit(self, eng, fn, waits=(), inc=None, n=1):
        for t in waits:
            if isinstance(t, list):
                for tt in t:
                    self.wait(eng, tt)
            else:
                self.wait(eng, t)
        if inc is None:
            self.thunks[eng].append(fn)
            return None
        inc.count += n
        tok = (inc, inc.count)
        self.thunks[eng].append(lambda e, f=fn, s=inc.h, nn=n: f(e).then_inc(s, nn))
        return tok

    def barrier(self):
        toks = [(s, s.count) for s in self.sems.values() if s.count > 0]
        for e in ENGS:
            for t in toks:
                self.wait(e, t)

    def flush(self):
        if not any(self.thunks[e] for e in ENGS):
            return
        with self.nc.Block() as block:
            for ename in ENGS:
                lst = self.thunks[ename]
                if lst:
                    getattr(block, ename)(lambda e, L=lst: [f(e) for f in L])
        self.thunks = {e: [] for e in ENGS}


def _c(x):
    return np.ascontiguousarray(x)


def make_consts(cfg):
    c = {}
    c["ident_f"] = np.eye(128, dtype=np.float32)
    c["ident_b"] = np.eye(128, dtype=BF)
    i64 = np.zeros((128, 64), dtype=BF)
    i64[:64, :] = np.eye(64, dtype=BF)
    i64[64:, :] = np.eye(64, dtype=BF)
    c["i64_2"] = i64
    c["ones1_b"] = np.ones((1, 128), dtype=BF)
    c["ones1_f"] = np.ones((1, 128), dtype=np.float32)
    c["ones128_f"] = np.ones((128, 1), dtype=np.float32)
    c["onesNCH_b"] = np.ones((cfg.NCH, 128), dtype=BF)
    m = np.zeros((2, 128, cfg.C), dtype=BF)
    for sh in range(2):
        s = np.arange(128)[:, None] + 128 * sh
        l = np.arange(cfg.C)[None, :]
        m[sh] = (l >= s).astype(BF)
    c["masks_b"] = m
    pmod = np.arange(128, dtype=np.uint64) % 8
    c["m15"] = (np.uint32(15) << (4 * pmod).astype(np.uint32)).astype(np.uint32).reshape(128, 1)
    c["minv"] = (2.0 ** -(4.0 * pmod)).astype(np.float32).reshape(128, 1)
    c["iota_gu"] = np.broadcast_to(np.arange(cfg.G_GU, dtype=np.float32),
                                   (128, cfg.G_GU)).copy()
    c["iota_d"] = np.broadcast_to(np.arange(cfg.G_D, dtype=np.float32),
                                  (128, cfg.G_D)).copy()
    vals = _fconst_vals(cfg)
    c["fconst"] = np.broadcast_to(np.asarray(vals, np.float32), (128, len(vals))).copy()
    return c


def _fconst_vals(cfg):
    return [0.0, 1.0, -1.0, float(cfg.eps_ln), float(cfg.eps_rms),
            1.0 / cfg.D, 1.0 / cfg.DI]


def make_in_maps(cfg, inp):
    x = np.asarray(inp["L0_attn_hidden_states"])
    w_in = np.asarray(inp["L0_attn_in_proj_w"])
    conv_w = np.asarray(inp["L0_attn_conv_w"])
    conv_b = np.asarray(inp["L0_attn_conv_b"])
    dt_bias = np.asarray(inp["L0_attn_dt_bias"])
    a_log = np.asarray(inp["L0_attn_A_log"])
    dvec = np.asarray(inp["L0_attn_D"])
    norm_w = np.asarray(inp["L0_attn_norm_w"])
    w_out = np.asarray(inp["L0_attn_out_proj_w"])
    qw_gu = np.asarray(inp["L0_ffn_qweight_gu"])
    qz_gu = np.asarray(inp["L0_ffn_qzeros_gu"])
    sc_gu = np.asarray(inp["L0_ffn_scales_gu"])
    gi_gu = np.asarray(inp["L0_ffn_g_idx_gu"])
    qw_d = np.asarray(inp["L0_ffn_qweight_down"])
    qz_d = np.asarray(inp["L0_ffn_qzeros_down"])
    sc_d = np.asarray(inp["L0_ffn_scales_down"])
    gi_d = np.asarray(inp["L0_ffn_g_idx_down"])

    M, KD = cfg.M_IN, cfg.KD
    JSLAB = min(1280, cfg.JTOT)
    nsl = (cfg.JTOT + JSLAB - 1) // JSLAB
    sc_sl = np.zeros((nsl, sc_gu.shape[0], JSLAB), np.float32)
    for si in range(nsl):
        o = si * JSLAB
        w = min(JSLAB, cfg.JTOT - o)
        sc_sl[si, :, :w] = sc_gu[:, o:o + w]
    sc_sl = sc_sl.reshape(-1, JSLAB)
    w_in_p = np.zeros((M * 128, cfg.D), dtype=w_in.dtype)
    w_in_p[:cfg.d_in] = w_in
    h_in = _c(w_in_p.reshape(M, 128, KD, 128).transpose(0, 3, 2, 1))
    shared = dict(
        h_in=h_in,
        woutT=_c(w_out.T),
        qw_gu=_c(np.repeat(qw_gu.reshape(KD, 16, cfg.JTOT), 8, axis=1)),
        qz_gu=_c(qz_gu), sc_gu=_c(sc_gu), sc_gu_sl=_c(sc_sl),
        gidx_gu=_c(gi_gu.reshape(KD, 128).T.astype(np.int32)),
        qw_d=_c(np.repeat(qw_d.reshape(cfg.KF, 16, cfg.D), 8, axis=1)),
        qz_d=_c(qz_d), sc_d=_c(sc_d),
        gidx_d=_c(gi_d.reshape(cfg.KF, 128).T.astype(np.int32)),
        conv_w=_c(conv_w.reshape(cfg.NXBC, 128, cfg.CK).transpose(1, 0, 2)
                  .reshape(128, cfg.NXBC * cfg.CK)),
        conv_b=_c(conv_b.reshape(cfg.NXBC, 128).T),
        dtb=_c(dt_bias.reshape(cfg.H, 1)),
        alog=_c(a_log.reshape(cfg.H, 1)),
        dvec=_c(dvec.reshape(1, cfg.H)),
        nw=_c(norm_w.reshape(cfg.KDI, 128).T),
        **make_consts(cfg),
    )
    in_maps = []
    for core in range(cfg.n_cores):
        bb, half = core // 2, core % 2
        m = dict(shared)
        xr = x.reshape(cfg.batch, -1, cfg.D)
        m["x"] = _c(xr[bb, half * cfg.T:(half + 1) * cfg.T, :])
        m["x_other"] = _c(xr[bb, (1 - half) * cfg.T:(2 - half) * cfg.T, :])
        m["flag"] = np.full((128, 1), float(half), np.float32)
        in_maps.append(m)
    return in_maps


def build(cfg):
    nc = bass.Bass("TRN2", target_bir_lowering=False)
    es = contextlib.ExitStack()
    nc._es = es
    c = cfg

    def din(name, shape, dt):
        return nc.dram_tensor(name, list(shape), dt, kind="ExternalInput").ap()

    x_d = din("x", (c.T, c.D), F32)
    x_other_d = din("x_other", (c.T, c.D), F32)
    h_in_d = din("h_in", (c.M_IN, 128, c.KD, 128), F32)
    woutT_d = din("woutT", (c.DI, c.D), F32)
    qw_gu_d = din("qw_gu", (c.KD, 128, c.JTOT), U32)
    qz_gu_d = din("qz_gu", (c.G_GU, c.JTOT // 8), U32)
    sc_gu_d = din("sc_gu", (c.G_GU, c.JTOT), F32)
    _JSL = min(1280, c.JTOT)
    _NSL = (c.JTOT + _JSL - 1) // _JSL
    sc_gu_sl_d = din("sc_gu_sl", (_NSL * c.G_GU, _JSL), F32)
    gidx_gu_d = din("gidx_gu", (128, c.KD), I32)
    qw_d_d = din("qw_d", (c.KF, 128, c.D), U32)
    qz_d_d = din("qz_d", (c.G_D, c.D // 8), U32)
    sc_d_d = din("sc_d", (c.G_D, c.D), F32)
    gidx_d_d = din("gidx_d", (128, c.KF), I32)
    conv_w_d = din("conv_w", (128, c.NXBC * c.CK), F32)
    conv_b_d = din("conv_b", (128, c.NXBC), F32)
    dtb_d = din("dtb", (c.H, 1), F32)
    alog_d = din("alog", (c.H, 1), F32)
    dvec_d = din("dvec", (1, c.H), F32)
    nw_d = din("nw", (128, c.KDI), F32)
    flag_d = din("flag", (128, 1), F32)
    ident_f_d = din("ident_f", (128, 128), F32)
    ident_b_d = din("ident_b", (128, 128), BF16)
    i64_2_d = din("i64_2", (128, 64), BF16)
    ones1_b_d = din("ones1_b", (1, 128), BF16)
    ones1_f_d = din("ones1_f", (1, 128), F32)
    ones128_f_d = din("ones128_f", (128, 1), F32)
    onesNCH_b_d = din("onesNCH_b", (c.NCH, 128), BF16)
    masks_b_d = din("masks_b", (2, 128, c.C), BF16)
    m15_d = din("m15", (128, 1), U32)
    minv_d = din("minv", (128, 1), F32)
    iota_gu_d = din("iota_gu", (128, c.G_GU), F32)
    iota_d_d = din("iota_d", (128, c.G_D), F32)
    fvals = _fconst_vals(c)
    fconst_d = din("fconst", (128, len(fvals)), F32)

    out_d = nc.dram_tensor("out", [c.T, c.D], F32, kind="ExternalOutput").ap()

    def dscr(name, shape, dt):
        return nc.dram_tensor(name, list(shape), dt, kind="Internal").ap()

    s_xn = dscr("s_xn", (c.T, c.D), BF16)
    s_z = dscr("s_z", (c.DI, c.T), BF16)
    s_xh = dscr("s_xh", (c.DI, c.T), BF16)
    s_bc = dscr("s_bc", (2 * c.GN, c.T), BF16)
    s_wgu = dscr("s_wgu", (c.KD, 128, c.JTOT), BF16)
    s_zgu = dscr("s_zgu", (2, c.G_GU, c.JTOT), BF16)
    s_h = dscr("s_h", (c.KF, 128, c.T), BF16)
    s_ygn = dscr("s_ygn", (c.KDI, 128, c.T), BF16)
    s_cs = dscr("s_cs", (c.H, c.T), F32)
    s_srecv = dscr("s_srecv", (128, c.H * c.P), BF16)
    s_rs = dscr("s_rs", (1, c.T), F32)
    s_u = dscr("s_u", (c.T, c.D), F32)
    s_dn = dscr("s_dn", (c.T, c.D), F32)

    b = B(nc, es)
    S = b.sem
    TE, DV, AC = S("te"), S("dve"), S("act")

    def sb(st, name, shape, dt, side=None):
        return st.enter_context(nc.sbuf_tensor(name, list(shape), dt, side=side))

    def pt(st, name, shape, dt=F32):
        return st.enter_context(nc.psum_tensor(name, list(shape), dt))

    ident_f = sb(es, "c_idf", [128, 128], F32)
    ident_b = sb(es, "c_idb", [128, 128], BF16)
    i64_2 = sb(es, "c_i64", [128, 64], BF16)
    ones1_b = sb(es, "c_o1b", [1, 128], BF16)
    ones1_f = sb(es, "c_o1f", [1, 128], F32)
    ones128_f = sb(es, "c_o128f", [128, 1], F32)
    onesNCH_b = sb(es, "c_oN", [c.NCH, 128], BF16)
    masks_b = sb(es, "c_mask", [128, 2 * c.C], BF16)
    m15 = sb(es, "c_m15", [128, 1], U32)
    minv = sb(es, "c_minv", [128, 1], F32)
    iota_gu = sb(es, "c_iog", [128, c.G_GU], F32)
    iota_d = sb(es, "c_iod", [128, c.G_D], F32)
    convw = sb(es, "c_cw", [128, c.NXBC * c.CK], F32)
    convb = sb(es, "c_cb", [128, c.NXBC], F32)
    dtb = sb(es, "c_dtb", [c.H, 1], F32)
    alog = sb(es, "c_alog", [c.H, 1], F32)
    dvec_s = sb(es, "c_dv", [1, c.H], F32)
    nw_s = sb(es, "c_nw", [128, c.KDI], F32)
    flag_s = sb(es, "c_flag", [128, 1], F32)
    gidx_gu_s = sb(es, "c_gig", [128, c.KD], I32)
    gidx_d_s = sb(es, "c_gid", [128, c.KF], I32)
    fconst_s = sb(es, "c_fc", [128, len(fvals)], F32)
    for j, v in enumerate(fvals):
        nc.const_aps.register(F32, v, fconst_s[:, j:j + 1])
    halo3 = sb(es, "halo3", [128, c.NXBC * 3], F32)

    CONST = S("cst")
    for dst, src in [(ident_f, ident_f_d), (ident_b, ident_b_d),
                     (i64_2, i64_2_d), (ones1_b, ones1_b_d),
                     (ones1_f, ones1_f_d), (ones128_f, ones128_f_d),
                     (onesNCH_b, onesNCH_b_d), (m15, m15_d), (minv, minv_d),
                     (iota_gu, iota_gu_d), (iota_d, iota_d_d),
                     (convw, conv_w_d), (convb, conv_b_d),
                     (dtb, dtb_d), (alog, alog_d), (dvec_s, dvec_d),
                     (nw_s, nw_d), (flag_s, flag_d),
                     (gidx_gu_s, gidx_gu_d), (gidx_d_s, gidx_d_d),
                     (fconst_s, fconst_d)]:
        b.emit("sync", lambda e, d=dst, s=src: e.dma_start(d[:], s[:]),
               inc=CONST, n=16)
    for sh in range(2):
        b.emit("sync", lambda e, sh=sh: e.dma_start(
            masks_b[:, sh * c.C:(sh + 1) * c.C], masks_b_d[sh, :, :]),
            inc=CONST, n=16)
    ctok = (CONST, CONST.count)
    for e in ENGS:
        b.wait(e, ctok)

    NT = c.T // 512
    KZ = c.KDI
    def emit_ln(tag, xsrc):
        esA = contextlib.ExitStack()
        xa = sb(esA, f"xa{tag}", [128, c.D], F32)
        xna = sb(esA, f"xna{tag}", [128, c.D], BF16)
        stat = sb(esA, f"stat{tag}", [128, 8], F32)
        XL = S(f"xld{tag}")
        XS = S(f"xnst{tag}")
        xa_free = None
        xna_free = None
        st_free = None
        for tt in range(c.TT):
            ld = b.emit("sync", lambda e, tt=tt: e.dma_start(
                xa[:], xsrc[tt * 128:(tt + 1) * 128, :]),
                waits=[xa_free], inc=XL, n=16)
            t1 = b.emit("vector", lambda e: e.reduce_sum(
                stat[:, 0:1], xa[:], axis=AX.X), waits=[ld, st_free], inc=DV)
            t2 = b.emit("vector", lambda e: e.tensor_scalar_mul(
                stat[:, 1:2], stat[:, 0:1], 1.0 / c.D), waits=[t1], inc=DV)
            t3 = b.emit("scalar", lambda e: e.activation(
                xna[:], xa[:], AF.Square), waits=[ld, xna_free], inc=AC)
            t4 = b.emit("vector", lambda e: e.reduce_sum(
                stat[:, 2:3], xna[:], axis=AX.X), waits=[t3], inc=DV)
            t4 = b.emit("vector", lambda e: e.tensor_scalar_mul(
                stat[:, 2:3], stat[:, 2:3], 1.0 / c.D), waits=[t4], inc=DV)
            t4b = b.emit("vector", lambda e: e.tensor_scalar(
                stat[:, 5:6], stat[:, 1:2], stat[:, 1:2], None, op0=ALU.mult),
                waits=[t2, t4], inc=DV)
            t4c = b.emit("vector", lambda e: e.tensor_sub(
                stat[:, 2:3], stat[:, 2:3], stat[:, 5:6]), waits=[t4b], inc=DV)
            t5a = b.emit("scalar", lambda e: e.activation(
                stat[:, 4:5], stat[:, 2:3], AF.Sqrt,
                bias=float(c.eps_ln)), waits=[t4c], inc=AC)
            t5 = b.emit("vector", lambda e: e.reciprocal(
                stat[:, 3:4], stat[:, 4:5]), waits=[t5a], inc=DV)
            t6 = b.emit("vector", lambda e: e.tensor_scalar(
                xna[:], xa[:], stat[:, 1:2], stat[:, 3:4],
                op0=ALU.subtract, op1=ALU.mult),
                waits=[t5, t4], inc=DV)
            xa_free = t6
            st_free = t6
            xna_free = b.emit("sync", lambda e, tt=tt: e.dma_start(
                s_xn[tt * 128:(tt + 1) * 128, :], xna[:]),
                waits=[t6], inc=XS, n=16)
        b.barrier()
        b.flush()
        esA.close()

    def emit_xnfm(tag, xn_fm):
        XT = S(f"xntr{tag}")
        for k in range(c.KD):
            b.emit("sync", lambda e, k=k: e.dma_start(
                xn_fm[:, k * c.T:(k + 1) * c.T],
                s_xn[:, k * 128:(k + 1) * 128], transpose=True), inc=XT, n=16)
        b.barrier()
        b.flush()

    def emit_inproj(tag, xn_fm, dt_raw, pre):
        esB = contextlib.ExitStack()
        wb = [sb(esB, f"wb{tag}{i}", [128, c.KD * 128], BF16) for i in range(2)]
        pB = [pt(esB, f"pB{tag}{i}", [128, c.T]) for i in range(2)]
        xbcb = sb(esB, f"xbcb{tag}", [128, c.T + 3], F32)
        cacc = sb(esB, f"cacc{tag}", [128, c.T], F32)
        ev_c = sb(esB, f"ev_c{tag}", [128, c.T], BF16)
        WL = [S(f"winld{tag}0"), S(f"winld{tag}1")]
        ZSTs, XHST, BCST = S(f"zst{tag}"), S(f"xhst{tag}"), S(f"bcst{tag}")
        wb_free = [None, None]
        pB_free = [None, None]
        ev_z_free = None
        ev_c_free = None
        xbc_free = []
        cacc_free = []
        halo_free = []
        if pre:
            z0 = b.emit("vector", lambda e: e.memset(xbcb[:, 0:3], 0.0), inc=DV)
        ms = list(range(KZ, c.M_IN)) if pre else list(range(c.M_IN))
        i = 0
        for m in ms:
            j = m - KZ
            halo_only = pre and (KZ + c.KDI + c.NB <= m < KZ + c.NXBC)
            rows = min(128, c.d_in - m * 128)
            wtok = b.emit("gpsimd", lambda e, i=i, m=m: e.dma_start(
                wb[i][:], h_in_d[m, :, :, :].rearrange("p k c -> p (k c)")),
                waits=[wb_free[i]], inc=WL[i], n=16)
            mmtok = None
            if halo_only:
                for k in range(c.KD):
                    mmtok = b.emit("tensor", lambda e, i=i, k=k: e.matmul(
                        pB[i][0:128, 0:3],
                        wb[i][:, k * 128:(k + 1) * 128],
                        xn_fm[:, k * c.T + c.T - 3:k * c.T + c.T],
                        start=(k == 0), stop=(k == c.KD - 1)),
                        waits=([wtok, pB_free[i]] if k == 0 else []),
                        inc=(TE if k == c.KD - 1 else None))
                wb_free[i] = mmtok
                hv = b.emit("vector", lambda e, i=i, j=j: e.tensor_scalar(
                    halo3[:, j * 3:(j + 1) * 3], pB[i][0:128, 0:3],
                    flag_s[:, 0:1], None, op0=ALU.mult),
                    waits=[mmtok] + halo_free, inc=DV)
                halo_free = []
                pB_free[i] = hv
                i = 1 - i
                continue
            for k in range(c.KD):
                for nb in range(NT):
                    mmtok = b.emit("tensor", lambda e, i=i, k=k, nb=nb, rows=rows: e.matmul(
                        pB[i][0:rows, nb * 512:(nb + 1) * 512],
                        wb[i][:, k * 128:k * 128 + rows],
                        xn_fm[:, k * c.T + nb * 512:k * c.T + (nb + 1) * 512],
                        start=(k == 0), stop=(k == c.KD - 1)),
                        waits=([wtok, pB_free[i]] if k == 0 and nb == 0 else []),
                        inc=(TE if (k == c.KD - 1 and nb == NT - 1) else None))
            wb_free[i] = mmtok
            if m < KZ:
                evv = b.emit("scalar", lambda e, i=i: e.activation(
                    ev_c[:], pB[i][:], AF.Silu), waits=[mmtok, ev_c_free], inc=AC)
                pB_free[i] = evv
                ev_c_free = b.emit("sync", lambda e, m=m: e.dma_start(
                    s_z[m * 128:(m + 1) * 128, :], ev_c[:]),
                    waits=[evv], inc=ZSTs, n=16)
            elif m < KZ + c.NXBC:
                evv = b.emit("scalar", lambda e, i=i: e.copy(
                    xbcb[:, 3:3 + c.T], pB[i][:]), waits=[mmtok] + xbc_free, inc=AC)
                pB_free[i] = evv
                if pre:
                    hv = b.emit("vector", lambda e, j=j: e.tensor_scalar(
                        halo3[:, j * 3:(j + 1) * 3], xbcb[:, c.T:c.T + 3],
                        flag_s[:, 0:1], None, op0=ALU.mult),
                        waits=[evv] + halo_free, inc=DV)
                    halo_free = []
                    h0tok = z0
                else:
                    hv = None
                    h0tok = b.emit("vector", lambda e, j=j: e.tensor_copy(
                        xbcb[:, 0:3], halo3[:, j * 3:(j + 1) * 3]),
                        waits=[evv] + xbc_free, inc=DV)
                t = b.emit("vector", lambda e, j=j: e.tensor_scalar_mul(
                    cacc[:], xbcb[:, 0:c.T], convw[:, j * c.CK:j * c.CK + 1]),
                    waits=[evv, h0tok] + cacc_free, inc=DV)
                for kk in range(1, c.CK):
                    t = b.emit("vector", lambda e, j=j, kk=kk: e.scalar_tensor_tensor(
                        cacc[:], xbcb[:, kk:kk + c.T],
                        convw[:, j * c.CK + kk:j * c.CK + kk + 1],
                        cacc[:], op0=ALU.mult, op1=ALU.add), waits=[t], inc=DV)
                xbc_free = [t] + ([hv] if hv else [])
                evc = b.emit("scalar", lambda e, j=j: e.activation(
                    ev_c[:], cacc[:], AF.Silu, bias=convb[:, j:j + 1]),
                    waits=[t, ev_c_free], inc=AC)
                cacc_free = [evc]
                if j < c.KDI:
                    ev_c_free = b.emit("sync", lambda e, j=j: e.dma_start(
                        s_xh[j * 128:(j + 1) * 128, :], ev_c[:]),
                        waits=[evc], inc=XHST, n=16)
                else:
                    jj = j - c.KDI
                    ev_c_free = b.emit("sync", lambda e, jj=jj: e.dma_start(
                        s_bc[jj * 128:(jj + 1) * 128, :], ev_c[:]),
                        waits=[evc], inc=BCST, n=16)
            else:
                evv = b.emit("scalar", lambda e, i=i, rows=rows: e.copy(
                    dt_raw[0:rows, :], pB[i][0:rows, :]), waits=[mmtok], inc=AC)
                pB_free[i] = evv
            i = 1 - i
        b.barrier()
        b.flush()
        esB.close()

    def emit_dtcs(tag, dt_raw, need_cumE, store_cs=False):
        es_dt = contextlib.ExitStack()
        csb = sb(es_dt, f"csb{tag}", [c.H, c.T], F32)
        acol = sb(es_dt, f"acol{tag}", [c.H, 1], F32)
        t_all = sb(es_dt, f"t_all{tag}", [c.H, c.NCH], F32)
        expT = sb(es_dt, f"expT{tag}", [c.H, c.NCH], F32)
        cumT = sb(es_dt, f"cumT{tag}", [c.H, c.NCH], F32)
        pT = pt(es_dt, f"pT{tag}", [128, 512])
        pT2 = pt(es_dt, f"pT2{tag}", [128, 512])
        spv = sb(es_dt, f"spv{tag}", [c.H, c.T], F32)
        spa = sb(es_dt, f"spa{tag}", [c.H, c.T], F32)
        rowb = sb(es_dt, f"rowb{tag}", [1, c.H], F32)
        t = b.emit("vector", lambda e: e.tensor_scalar_add(
            spv[:], dt_raw[:], dtb[:]), inc=DV)
        t1 = b.emit("scalar", lambda e: e.activation(spa[:], spv[:], AF.Abs),
                    waits=[t], inc=AC)
        t1 = b.emit("scalar", lambda e: e.activation(
            spa[:], spa[:], AF.Exp, scale=-1.0), waits=[t1], inc=AC)
        t1 = b.emit("vector", lambda e: e.tensor_scalar_add(spa[:], spa[:], 1.0),
                    waits=[t1], inc=DV)
        t1 = b.emit("scalar", lambda e: e.activation(spa[:], spa[:], AF.Ln),
                    waits=[t1], inc=AC)
        t2 = b.emit("scalar", lambda e: e.activation(dt_f[:], spv[:], AF.Relu),
                    waits=[t, t1], inc=AC)
        t = b.emit("vector", lambda e: e.tensor_add(dt_f[:], dt_f[:], spa[:]),
                   waits=[t1, t2], inc=DV)
        ta = b.emit("scalar", lambda e: e.activation(acol[:], alog[:], AF.Exp),
                    inc=AC)
        ta = b.emit("vector", lambda e: e.tensor_scalar_mul(acol[:], acol[:], -1.0),
                    waits=[ta], inc=DV)
        t = b.emit("vector", lambda e: e.tensor_scalar_mul(cs[:], dt_f[:], acol[:]),
                   waits=[t, ta], inc=DV)
        srcb, dstb = cs, csb
        step = 1
        while step < c.C:
            last = None
            for ch in range(c.NCH):
                o = ch * c.C
                b.emit("vector", lambda e, s=srcb, d=dstb, o=o, st=step: e.tensor_copy(
                    d[:, o:o + st], s[:, o:o + st]), waits=[t], inc=DV)
                last = b.emit("vector", lambda e, s=srcb, d=dstb, o=o, st=step: e.tensor_add(
                    d[:, o + st:o + c.C], s[:, o + st:o + c.C], s[:, o:o + c.C - st]),
                    waits=[t], inc=DV)
            t = last
            srcb, dstb = dstb, srcb
            step *= 2
        if srcb is not cs:
            t = b.emit("vector", lambda e: e.tensor_copy(cs[:], csb[:]),
                       waits=[t], inc=DV)
        cs_tok = t
        cs_st_tok = None
        if store_cs:
            CSST = S(f"csst{tag}")
            cs_st_tok = b.emit("sync", lambda e: e.dma_start(s_cs[:], cs[:]),
                               waits=[cs_tok], inc=CSST, n=16)
        tt_toks = []
        for ch in range(c.NCH):
            tt_toks.append(b.emit("vector", lambda e, ch=ch: e.tensor_copy(
                t_all[:, ch:ch + 1], cs[:, ch * c.C + c.C - 1:ch * c.C + c.C]),
                waits=[cs_tok], inc=DV))
        t = b.emit("scalar", lambda e: e.activation(expT[:], t_all[:], AF.Exp),
                   waits=tt_toks, inc=AC)

        def bcast_cols(src_fm, dest_sb, wtok):
            last = None
            prev_cp = None
            prev_mm = None
            for ch in range(c.NCH):
                tr = b.emit("tensor", lambda e, ch=ch, s=src_fm: e.transpose(
                    pT[0:1, 0:c.H], s[:, ch:ch + 1], ident_f[0:c.H, 0:c.H]),
                    waits=wtok + [prev_cp], inc=TE)
                cp = b.emit("scalar", lambda e: e.copy(
                    rowb[0:1, 0:c.H], pT[0:1, 0:c.H]), waits=[tr, prev_mm], inc=AC)
                prev_cp = cp
                mm = b.emit("tensor", lambda e: e.matmul(
                    pT2[0:128, 0:c.H], ones1_f[0:1, :], rowb[0:1, 0:c.H]),
                    waits=[cp, last], inc=TE)
                prev_mm = mm
                last = b.emit("vector", lambda e, ch=ch, d=dest_sb: e.tensor_copy(
                    d[:, ch * c.H:(ch + 1) * c.H], pT2[0:128, 0:c.H]),
                    waits=[mm], inc=DV)
            return last

        ea_tok = bcast_cols(expT, e_all, [t])
        cumE_tok = None
        if need_cumE:
            prev = b.emit("vector", lambda e: e.memset(cumT[:, 0:1], 0.0), inc=DV)
            for ch in range(1, c.NCH):
                prev = b.emit("vector", lambda e, ch=ch: e.tensor_add(
                    cumT[:, ch:ch + 1], cumT[:, ch - 1:ch], t_all[:, ch - 1:ch]),
                    waits=[prev] + tt_toks, inc=DV)
            tcE = b.emit("scalar", lambda e: e.activation(cumT[:], cumT[:], AF.Exp),
                         waits=[prev], inc=AC)
            ce_tok = bcast_cols(cumT, cumE_all, [tcE, ea_tok])
            cumE_tok = b.emit("vector", lambda e: e.tensor_scalar_mul(
                cumE_all[:], cumE_all[:], flag_s[:, 0:1]), waits=[ce_tok], inc=DV)
        b.barrier()
        b.flush()
        es_dt.close()
        return cs_tok, cs_st_tok, ea_tok, cumE_tok

    def make_ssd_ctx(st, tag, ptr=None):
        d = {}
        d["xtk"] = [[sb(st, f"xtk{tag}{sh}", [128, c.DI], BF16) for sh in range(2)]]
        d["bcb"] = [sb(st, f"bcb{tag}", [128, 2 * c.NB * c.C], BF16)]
        d["cstk"] = sb(st, f"cstk{tag}", [128, 2 * c.H], F32)
        d["ncstk"] = sb(st, f"ncstk{tag}", [128, 2 * c.H], F32)
        d["dttk"] = sb(st, f"dttk{tag}", [128, 2 * c.H], F32)
        d["ddmf"] = sb(st, f"ddmf{tag}", [c.H, c.C], F32)
        d["ddmtk"] = sb(st, f"ddmtk{tag}", [128, 2 * c.H], F32)
        d["btokb"] = sb(st, f"btokb{tag}", [128, 256], BF16)
        d["bdb"] = sb(st, f"bdb{tag}", [128, 512], BF16)
        d["S_a"] = sb(st, f"S_a{tag}", [128, c.H * c.P], F32)
        d["pS"] = [pt(st, f"pS{tag}{i}", [128, 512]) for i in range(2)]
        d["pTR"] = ptr if ptr is not None else pt(st, f"pTR{tag}", [128, 512])
        d["pTRB"] = pt(st, f"pTRB{tag}", [128, 128], BF16)
        d["XTK"] = [S(f"xtl{tag}0"), S(f"xtl{tag}1")]
        d["BCL"] = S(f"bcl{tag}")
        d["xtk_free"] = [None, None]
        d["bcb_free"] = [None]
        d["ptr_free"] = [None]
        d["ptrb_free"] = [None]
        d["ddmf_free"] = [[]]

        def load_chunk(ch, b_only=False):
            o = ch * c.C
            xts = [b.emit("sync", lambda e, sh=sh, o=o: e.dma_start(
                d["xtk"][0][sh][:], s_xh[:, o + sh * 128:o + (sh + 1) * 128],
                transpose=True), waits=[d["xtk_free"][sh]], inc=d["XTK"][sh], n=16)
                for sh in range(2)]
            bt = None
            for g in range(c.NB if b_only else 2 * c.NB):
                bt = b.emit("sync", lambda e, g=g, o=o: e.dma_start(
                    d["bcb"][0][:, g * c.C:(g + 1) * c.C],
                    s_bc[g * 128:(g + 1) * 128, o:o + c.C]),
                    waits=[d["bcb_free"][0]] if g == 0 else [], inc=d["BCL"], n=16)
            return xts, [bt]

        def chunk_trans(ch, waits):
            o = ch * c.C
            out = []
            t1 = b.emit("scalar", lambda e, o=o: e.activation(
                d["ddmf"][:], cs[:, o:o + c.C], AF.Exp,
                bias=cs[:, o + c.C - 1:o + c.C], scale=-1.0),
                waits=waits + d["ddmf_free"][0], inc=AC)
            t1 = b.emit("vector", lambda e, o=o: e.tensor_mul(
                d["ddmf"][:], d["ddmf"][:], dt_f[:, o:o + c.C]), waits=[t1], inc=DV)
            ddmf_readers = [t1]
            for sh in range(2):
                oo = o + sh * 128
                a1 = b.emit("tensor", lambda e, oo=oo: e.transpose(
                    d["pTR"][:, 0:c.H], cs[:, oo:oo + 128], ident_f[0:c.H, 0:c.H]),
                    waits=waits + [d["ptr_free"][0]], inc=TE)
                a2 = b.emit("vector", lambda e, sh=sh: e.tensor_copy(
                    d["cstk"][:, sh * c.H:(sh + 1) * c.H], d["pTR"][0:128, 0:c.H]),
                    waits=[a1], inc=DV)
                a3 = b.emit("vector", lambda e, sh=sh: e.tensor_scalar_mul(
                    d["ncstk"][:, sh * c.H:(sh + 1) * c.H],
                    d["cstk"][:, sh * c.H:(sh + 1) * c.H], -1.0), waits=[a2], inc=DV)
                a4 = b.emit("tensor", lambda e, oo=oo: e.transpose(
                    d["pTR"][:, 128:128 + c.H], dt_f[:, oo:oo + 128],
                    ident_f[0:c.H, 0:c.H]), waits=[a2], inc=TE)
                a5 = b.emit("vector", lambda e, sh=sh: e.tensor_copy(
                    d["dttk"][:, sh * c.H:(sh + 1) * c.H],
                    d["pTR"][0:128, 128:128 + c.H]), waits=[a4], inc=DV)
                a6 = b.emit("tensor", lambda e, sh=sh: e.transpose(
                    d["pTR"][:, 256:256 + c.H], d["ddmf"][:, sh * 128:(sh + 1) * 128],
                    ident_f[0:c.H, 0:c.H]), waits=[t1, a5], inc=TE)
                ddmf_readers.append(a6)
                a7 = b.emit("vector", lambda e, sh=sh: e.tensor_copy(
                    d["ddmtk"][:, sh * c.H:(sh + 1) * c.H],
                    d["pTR"][0:128, 256:256 + c.H]), waits=[a6], inc=DV)
                out += [a3, a5, a7]
            d["ptr_free"][0] = out[-1]
            d["ddmf_free"][0] = ddmf_readers
            return out

        d["load_chunk"] = load_chunk
        d["chunk_trans"] = chunk_trans
        return d

    def emit_scan(ctx, tag, cs_tok, ea_tok, store_states):
        S_a = ctx["S_a"]
        pS = ctx["pS"][0]
        st_stage = None
        SST = None
        if store_states:
            st_stage = ctx["st_stage"]
            SST = S(f"stst{tag}")
        tz = b.emit("vector", lambda e: e.memset(S_a[:], 0.0), inc=DV)
        s_cur_tok = tz
        pS_free = []
        st_w_free = None
        bd_free = []
        for ch in range(c.NCH):
            xts, bts = ctx["load_chunk"](ch, b_only=not store_states)
            ctoks = ctx["chunk_trans"](ch, [cs_tok])
            st_rd = []
            for g in range(c.G):
                gB = g * (c.NST // 128)
                trts = []
                for sh in range(2):
                    u1 = b.emit("tensor", lambda e, gB=gB, sh=sh: e.transpose(
                        ctx["pTRB"][:, 0:128],
                        ctx["bcb"][0][:, gB * c.C + sh * 128:gB * c.C + (sh + 1) * 128],
                        ident_b[:]), waits=bts + st_rd + [ctx["ptrb_free"][0]], inc=TE)
                    u2 = b.emit("vector", lambda e, sh=sh: e.tensor_copy(
                        ctx["btokb"][:, sh * 128:(sh + 1) * 128],
                        ctx["pTRB"][0:128, 0:128]), waits=[u1], inc=DV)
                    ctx["ptrb_free"][0] = u2
                    trts.append(u2)
                for hh in range(c.H // c.G):
                    h = g * (c.H // c.G) + hh
                    v1 = b.emit("vector", lambda e, h=h: e.tensor_scalar(
                        ctx["bdb"][:, 0:128], ctx["btokb"][:, 0:128],
                        ctx["ddmtk"][:, h:h + 1], None, op0=ALU.mult),
                        waits=trts + ctoks + bd_free, inc=DV)
                    v2 = b.emit("vector", lambda e, h=h: e.tensor_scalar(
                        ctx["bdb"][:, 128:256], ctx["btokb"][:, 128:256],
                        ctx["ddmtk"][:, c.H + h:c.H + h + 1], None, op0=ALU.mult),
                        waits=trts + ctoks + bd_free, inc=DV)
                    m1 = b.emit("tensor", lambda e, h=h: e.matmul(
                        pS[:, 0:c.P], ctx["bdb"][:, 0:128],
                        ctx["xtk"][0][0][:, h * c.P:(h + 1) * c.P],
                        start=True, stop=False),
                        waits=[v1] + xts + pS_free, inc=None)
                    m2 = b.emit("tensor", lambda e, h=h: e.matmul(
                        pS[:, 0:c.P], ctx["bdb"][:, 128:256],
                        ctx["xtk"][0][1][:, h * c.P:(h + 1) * c.P],
                        start=False, stop=True), waits=[v2], inc=TE)
                    bd_free = [m2]
                    e2 = b.emit("vector", lambda e, h=h, ch=ch: e.scalar_tensor_tensor(
                        S_a[:, h * c.P:(h + 1) * c.P],
                        S_a[:, h * c.P:(h + 1) * c.P],
                        e_all[:, ch * c.H + h:ch * c.H + h + 1],
                        pS[:, 0:c.P], op0=ALU.mult, op1=ALU.add),
                        waits=[m2, s_cur_tok, ea_tok], inc=DV)
                    pS_free = [e2]
                    st_rd.append(e2)
                    if store_states:
                        e1 = b.emit("scalar", lambda e, h=h: e.copy(
                            st_stage[:, h * c.P:(h + 1) * c.P],
                            S_a[:, h * c.P:(h + 1) * c.P]),
                            waits=[e2, st_w_free], inc=AC)
                        st_rd.append(e1)
            if store_states:
                stt = b.emit("sync", lambda e, ch=ch: e.dma_start(
                    s_states[ch, :, :], st_stage[:]), waits=st_rd, inc=SST, n=16)
                st_w_free = stt
                gate = stt
            else:
                gate = st_rd[-1]
                for tk in st_rd:
                    pass
            for sh in range(2):
                ctx["xtk_free"][sh] = gate
            ctx["bcb_free"][0] = gate
            s_cur_tok = gate if store_states else st_rd[-1]
        return s_cur_tok, S_a

    esP2 = contextlib.ExitStack()
    dt_raw_p = sb(esP2, "dt_raw_p", [c.H, c.T], F32, side="right")
    esP1 = contextlib.ExitStack()
    xn_fm_p = sb(esP1, "xn_fm_p", [128, c.KD * c.T], BF16, side="right")
    emit_ln("p", x_other_d)
    emit_xnfm("p", xn_fm_p)
    emit_inproj("p", xn_fm_p, dt_raw_p, pre=True)
    esP1.close()
    es_mid = contextlib.ExitStack()
    dt_f = sb(es_mid, "dt_f", [c.H, c.T], F32)
    cs = sb(es_mid, "cs", [c.H, c.T], F32)
    e_all = sb(es_mid, "e_all", [128, c.NCH * c.H], F32)
    cumE_all = sb(es_mid, "cumE", [128, c.NCH * c.H], F32)
    cs_tok_p, _, ea_tok_p, _ = emit_dtcs("p", dt_raw_p, need_cumE=False)
    esP2.close()
    esPS = contextlib.ExitStack()
    ctx_p = make_ssd_ctx(esPS, "p")
    sr_stg = sb(esPS, "sr_stg", [128, c.H * c.P], BF16)
    sfin_tok, Sfin = emit_scan(ctx_p, "p", cs_tok_p, ea_tok_p, store_states=False)
    srt = b.emit("vector", lambda e: e.tensor_scalar(
        sr_stg[:], Sfin[:], flag_s[:, 0:1], None, op0=ALU.mult),
        waits=[sfin_tok], inc=DV)
    SRST = S("srst")
    b.emit("sync", lambda e: e.dma_start(s_srecv[:], sr_stg[:]),
           waits=[srt], inc=SRST, n=16)
    b.barrier()
    b.flush()
    esPS.close()

    es_xn = contextlib.ExitStack()
    xn_fm = sb(es_xn, "xn_fm", [128, c.KD * c.T], BF16, side="right")
    es_dtr = contextlib.ExitStack()
    dt_raw = sb(es_dtr, "dt_raw", [c.H, c.T], F32, side="right")
    emit_ln("m", x_d)
    emit_xnfm("m", xn_fm)
    emit_inproj("m", xn_fm, dt_raw, pre=False)
    cs_tok, cs_st_tok, ea_tok, cumE_tok = emit_dtcs("m", dt_raw, need_cumE=False,
                                                     store_cs=True)
    es_dtr.close()
    esG2 = contextlib.ExitStack()
    xg_hi = sb(esG2, "xg_hi", [c.G_GU, c.T], BF16)
    xg_lo = sb(esG2, "xg_lo", [c.G_GU, c.T], BF16)
    esG2a = contextlib.ExitStack()
    ZSL = min(1024, c.JTOT)
    zslabs = []
    o = 0
    while o < c.JTOT:
        w = min(ZSL, c.JTOT - o)
        zslabs.append((o, w))
        o += w
    zq_s = sb(esG2a, "zqs", [c.G_GU, ZSL // 8], U32)
    zf = sb(esG2a, "zf", [c.G_GU, ZSL], F32)
    ztu = sb(esG2a, "ztu", [c.G_GU, ZSL // 8], U32)
    ztf = sb(esG2a, "ztf", [c.G_GU, ZSL // 8], F32)
    scz = sb(esG2a, "scz", [c.G_GU, ZSL], F32)
    znst = sb(esG2a, "znst", [c.G_GU, ZSL], BF16)
    oh = sb(esG2a, "oh", [128, c.KD * c.G_GU], BF16)
    gxf = sb(esG2a, "gxf", [128, c.KD], F32)
    xg_f = sb(esG2a, "xg_f", [c.G_GU, c.T], F32)
    pX = pt(esG2a, "pX", [c.G_GU, c.T])
    ZL = S("zl")
    ZQL = S("zql")
    ZST2 = S("zst2")
    zn_st_last = None
    prevs = []
    zq_prev = []
    for (o, w) in zslabs:
        zqt = b.emit("sync", lambda e, o=o, w=w: e.dma_start(
            zq_s[:, 0:w // 8], qz_gu_d[:, o // 8:(o + w) // 8]),
            waits=zq_prev, inc=ZQL, n=16)
        lsc = b.emit("sync", lambda e, o=o, w=w: e.dma_start(
            scz[:, 0:w], sc_gu_d[:, o:o + w]), waits=prevs, inc=ZL, n=16)
        ztoks = []
        for sft in range(8):
            u1 = b.emit("vector", lambda e, o=o, w=w, sft=sft: e.tensor_scalar(
                ztu[:, 0:w // 8], zq_s[:, 0:w // 8],
                (15 << (4 * sft)), None, op0=ALU.bitwise_and),
                waits=[zqt] + ztoks[-3:] + prevs, inc=DV)
            u2 = b.emit("vector", lambda e, w=w: e.tensor_copy(
                ztf[:, 0:w // 8], ztu[:, 0:w // 8]), waits=[u1], inc=DV)
            u3 = b.emit("vector", lambda e, w=w, sft=sft: e.tensor_scalar_mul(
                zf[:, sft:w:8], ztf[:, 0:w // 8],
                float(2.0 ** (-4 * sft))), waits=[u2], inc=DV)
            ztoks += [u1, u2, u3]
        t = b.emit("vector", lambda e, w=w: e.tensor_scalar_add(
            zf[:, 0:w], zf[:, 0:w], 1.0), waits=ztoks, inc=DV)
        t = b.emit("vector", lambda e, w=w: e.tensor_mul(
            zf[:, 0:w], zf[:, 0:w], scz[:, 0:w]), waits=[t, lsc], inc=DV)
        t = b.emit("vector", lambda e, w=w: e.tensor_scalar_mul(
            zf[:, 0:w], zf[:, 0:w], -1.0), waits=[t], inc=DV)
        th = b.emit("vector", lambda e, w=w: e.tensor_copy(
            znst[:, 0:w], zf[:, 0:w]), waits=[t, zn_st_last], inc=DV)
        st1 = b.emit("sync", lambda e, o=o, w=w: e.dma_start(
            s_zgu[0, :, o:o + w], znst[:, 0:w]), waits=[th], inc=ZST2, n=16)
        t2 = b.emit("vector", lambda e, w=w: e.tensor_sub(
            scz[:, 0:w], zf[:, 0:w], znst[:, 0:w]), waits=[th], inc=DV)
        tl = b.emit("vector", lambda e, w=w: e.tensor_copy(
            znst[:, 0:w], scz[:, 0:w]), waits=[t2, st1], inc=DV)
        st2 = b.emit("sync", lambda e, o=o, w=w: e.dma_start(
            s_zgu[1, :, o:o + w], znst[:, 0:w]), waits=[tl], inc=ZST2, n=16)
        zn_st_last = st2
        prevs = [t2, st2]
        zq_prev = ztoks[-3:-2]
    zn_tok = zn_st_last
    tg = b.emit("vector", lambda e: e.tensor_copy(gxf[:], gidx_gu_s[:]), inc=DV)
    ohtoks = []
    mmt = None
    for k in range(c.KD):
        otk = b.emit("vector", lambda e, k=k: e.tensor_scalar(
            oh[:, k * c.G_GU:(k + 1) * c.G_GU], iota_gu[:], gxf[:, k:k + 1],
            None, op0=ALU.is_equal), waits=[tg], inc=DV)
        ohtoks.append(otk)
    for k in range(c.KD):
        for nb in range(NT):
            mmt = b.emit("tensor", lambda e, k=k, nb=nb: e.matmul(
                pX[:, nb * 512:(nb + 1) * 512],
                oh[:, k * c.G_GU:(k + 1) * c.G_GU],
                xn_fm[:, k * c.T + nb * 512:k * c.T + (nb + 1) * 512],
                start=(k == 0), stop=(k == c.KD - 1)),
                waits=[ohtoks[k]], inc=(TE if k == c.KD - 1 and nb == NT - 1 else None))
    t = b.emit("vector", lambda e: e.tensor_copy(xg_f[:], pX[:]), waits=[mmt], inc=DV)
    t = b.emit("vector", lambda e: e.tensor_copy(xg_hi[:], xg_f[:]), waits=[t], inc=DV)
    t2 = b.emit("vector", lambda e: e.tensor_sub(xg_f[:], xg_f[:], xg_hi[:]),
                waits=[t], inc=DV)
    xg_tok = b.emit("vector", lambda e: e.tensor_copy(xg_lo[:], xg_f[:]),
                    waits=[t2], inc=DV)
    b.barrier()
    b.flush()
    esG2a.close()
    es_xn.close()

    esG1 = contextlib.ExitStack()
    JSLAB = min(1280, c.JTOT)
    slabs = []
    o = 0
    while o < c.JTOT:
        w = min(JSLAB, c.JTOT - o)
        slabs.append((o, w))
        o += w
    qwb = [sb(esG1, f"qwb{i}", [128, JSLAB], U32) for i in range(2)]
    scb = [sb(esG1, f"scb{i}", [128, JSLAB], BF16) for i in range(2)]
    nib = [sb(esG1, f"nib{i}", [128, JSLAB], BF16) for i in range(2)]
    wsl = [sb(esG1, f"wsl{i}", [128, JSLAB], BF16) for i in range(2)]
    QWL = [S("qwl0"), S("qwl1")]
    SCL = [S("scl0"), S("scl1")]
    WST = [S("wst0"), S("wst1")]
    PL_G1 = S("plg1")
    g1_state = dict(qw_free=[None, None], sc_free=[None, None],
                    wsl_free=[None, None], nib_free=[None, None], u=0)

    g1_units = [(k, si, o, w) for k in range(c.KD)
                for si, (o, w) in enumerate(slabs)]
    g1_pos = [0]
    g1_lpos = [0]
    g1_ld = {}

    def g1_load(j):
        k, si, o, w = g1_units[j]
        i = j % 2
        lq = b.emit("gpsimd", lambda e, i=i, k=k, o=o, w=w: e.dma_start(
            qwb[i][:, 0:w], qw_gu_d[k, :, o:o + w]),
            waits=[g1_state["qw_free"][i]], inc=QWL[i], n=16)
        ls = b.emit("gpsimd", lambda e, i=i, k=k, si=si: e.indirect_dma_start(
            scb[i][:], None, sc_gu_sl_d[:, :],
            IOA(gidx_gu_s[:, k:k + 1], 0),
            element_offset=si * c.G_GU * _JSL),
            waits=[g1_state["sc_free"][i]], inc=SCL[i], n=16)
        g1_ld[j] = (lq, ls)

    def g1_comp(j):
        k, si, o, w = g1_units[j]
        i = j % 2
        lq, ls = g1_ld.pop(j)
        v1 = b.emit("vector", lambda e, i=i, w=w: e.tensor_scalar(
            qwb[i][:, 0:w], qwb[i][:, 0:w], m15[:], None,
            op0=ALU.bitwise_and), waits=[lq], inc=DV)
        v2 = b.emit("scalar", lambda e, i=i, w=w: e.mul(
            nib[i][:, 0:w], qwb[i][:, 0:w], minv[:]),
            waits=[v1, g1_state["nib_free"][i]], inc=AC)
        g1_state["qw_free"][i] = v2
        v5 = b.emit("gpsimd", lambda e, i=i, w=w: e.tensor_mul(
            wsl[i][:, 0:w], nib[i][:, 0:w], scb[i][:, 0:w]),
            waits=[v2, ls, g1_state["wsl_free"][i]], inc=PL_G1)
        g1_state["nib_free"][i] = v5
        g1_state["sc_free"][i] = v5
        g1_state["wsl_free"][i] = b.emit("sync",
            lambda e, i=i, k=k, o=o, w=w: e.dma_start(
                s_wgu[k, :, o:o + w], wsl[i][:, 0:w]), waits=[v5],
            inc=WST[i], n=16)

    def g1_run(n):
        end = min(len(g1_units), g1_pos[0] + n)
        for j in range(g1_pos[0], end):
            while g1_lpos[0] < min(len(g1_units), j + 2):
                g1_load(g1_lpos[0])
                g1_lpos[0] += 1
            g1_comp(j)
        g1_pos[0] = end

    es_ssd = contextlib.ExitStack()
    pSC = pt(es_ssd, "pSC", [128, 512])
    ctx_m = make_ssd_ctx(es_ssd, "m", ptr=pSC)
    xtk = ctx_m["xtk"]
    bcb = ctx_m["bcb"]
    cstk, ncstk = ctx_m["cstk"], ctx_m["ncstk"]
    dttk, ddmtk = ctx_m["dttk"], ctx_m["ddmtk"]
    btokb, bdb = ctx_m["btokb"], ctx_m["bdb"]
    S_f32 = ctx_m["S_a"]
    pS = ctx_m["pS"]
    load_chunk0 = ctx_m["load_chunk"]
    chunk_trans0 = ctx_m["chunk_trans"]
    xtk_free = ctx_m["xtk_free"]
    bcb_free = ctx_m["bcb_free"]

    esM = contextlib.ExitStack()
    S_recv = sb(esM, "S_recv", [128, c.H * c.P], BF16)
    srun_bf = sb(esM, "srun_bf", [128, c.H * c.P], BF16)
    zjit = [sb(esM, f"zjit{i}", [128, c.C], BF16) for i in range(2)]
    xjit = [sb(esM, f"xjit{i}", [128, c.C], BF16) for i in range(2)]
    csb = [sb(esM, f"csb{i}", [128, c.C], F32) for i in range(4)]
    etmp = [sb(esM, f"etmp{i}", [128, 2 * c.C], F32) for i in range(2)]
    scsb = sb(esM, "scsb", [128, 2 * c.C], BF16)
    ebuf = [sb(esM, f"ebuf{i}", [128, 2 * c.C], BF16) for i in range(2)]
    odb = [sb(esM, f"odb{i}", [128, c.C], BF16) for i in range(2)]
    cdb = [sb(esM, f"cdb{i}", [128, c.C], BF16) for i in range(2)]
    gtb = [sb(esM, f"gtb{i}", [128, 2 * c.C], BF16) for i in range(2)]
    ygb = [sb(esM, f"ygb{i}", [128, c.C], BF16) for i in range(2)]
    ygnb = [sb(esM, f"ygnb{i}", [128, c.C], BF16) for i in range(2)]
    sqb = [sb(esM, f"sqb{i}", [128, c.C], F32) for i in range(2)]
    rmsacc = sb(esM, "rmsacc", [128, c.C], F32)
    rs_piece = sb(esM, "rs_piece", [1, c.C], F32)
    RSST = S("rsst")
    dball_all = sb(esM, "dball_all", [128, (c.H // 2) * c.P], BF16)
    dbc = sb(esM, "dbc", [128, c.H], F32)
    dvb = sb(esM, "dvb", [1, c.H], BF16)
    mng = sb(esM, "mng", [128, 2 * c.C], BF16)
    pY = [pt(esM, f"pY{i}", [128, 512]) for i in range(4)]
    ZJ = [S(f"zj{q}") for q in range(2)]
    XJ = [S(f"xj{q}") for q in range(2)]
    YGST = [S("ygst0"), S("ygst1")]
    CSL = [S(f"csl{q}") for q in range(4)]

    mng_tok = b.emit("vector", lambda e: e.tensor_scalar(
        mng[:], masks_b[:], -1.0, 1.0e9, op0=ALU.add, op1=ALU.mult), inc=DV)
    RB = S("rcv")
    srecv_tok = b.emit("sync", lambda e: e.dma_start(S_recv[:], s_srecv[:]),
                       inc=RB, n=16)
    sinit_tok = b.emit("vector", lambda e: e.tensor_copy(S_f32[:], S_recv[:]),
                       waits=[srecv_tok], inc=DV)
    t = b.emit("vector", lambda e: e.tensor_copy(dvb[:], dvec_s[:]), inc=DV)
    t = b.emit("tensor", lambda e: e.matmul(pY[0][:, 0:c.H], ones1_b[:], dvb[:]),
               waits=[t], inc=TE)
    t = b.emit("vector", lambda e: e.tensor_copy(dbc[:], pY[0][:, 0:c.H]),
               waits=[t], inc=DV)
    dbc_tok = t
    dbc_psum_tok = t
    DBA = S("dba")
    dba_tok = None
    for tdb in range(c.H // 2):
        dba_tok = b.emit("vector", lambda e, t=tdb: e.tensor_scalar(
            dball_all[0:64, t * c.P:(t + 1) * c.P], i64_2[0:64, :],
            dbc[0:64, 2 * t:2 * t + 1], None, op0=ALU.mult),
            waits=[dbc_tok], inc=DBA)
        dba_tok = b.emit("vector", lambda e, t=tdb: e.tensor_scalar(
            dball_all[64:128, t * c.P:(t + 1) * c.P], i64_2[64:128, :],
            dbc[64:128, 2 * t + 1:2 * t + 2], None, op0=ALU.mult),
            waits=[dbc_tok], inc=DBA)

    HP2 = 128 // c.P
    HG = c.H // c.G
    zj_free = [None] * 2
    xj_free = [None] * 2
    yg_free = [None, None]
    ygn_free = [None, None]
    pY_free = [dbc_psum_tok, None, None, None]
    scs_free = None
    csb_free = [None, None, None, None]
    csb_tok = [None, None, None, None]
    etmp_free = [None, None]
    eb_free = [None, None]
    od_free = [None, None]
    cd_free = [None, None]
    gt_free = [None, None]
    pS_free = [None, None]
    bd_free = []
    bdb_rd = [None, None]
    sqb_free = [None, None]
    rs_st_prev = None
    st = {"supd_last": sinit_tok, "sq_prev": None, "rms_mm_prev": None,
          "first_sq": True}
    srun_readers = [sinit_tok]
    r2 = None
    PL = S("plc")
    pending_supd = [None]
    pending_pair = [None]

    def flush_supd():
        if pending_supd[0] is not None:
            fn = pending_supd[0]
            pending_supd[0] = None
            fn()

    def flush_pair():
        if pending_pair[0] is not None:
            fn = pending_pair[0]
            pending_pair[0] = None
            fn()

    for ch in range(c.NCH):
        xts, bts = load_chunk0(ch)
        ctx_m["ptr_free"][0] = scs_free
        ctoks = chunk_trans0(ch, [cs_tok])
        cast_tok = b.emit("vector", lambda e: e.tensor_copy(srun_bf[:], S_f32[:]),
                          waits=[st["supd_last"]] + srun_readers, inc=DV)
        srun_readers = []
        st["first_sq"] = True
        xl = None

        def csb_load(h, ch=ch):
            j = h % 4
            csb_tok[j] = b.emit("sync", lambda e, h=h, j=j, ch=ch: e.dma_start(
                csb[j][:], s_cs[h:h + 1, ch * c.C:(ch + 1) * c.C]
                .broadcast_to([128, c.C])),
                waits=[cs_st_tok] + (csb_free[j] or []), inc=CSL[j], n=16)

        for h0 in range(3):
            csb_load(h0)
        for g in range(c.G):
            gB = g * (c.NST // 128)
            gC = c.NB + gB
            sc_toks = []
            qprev = None
            for sh in range(2):
                q1 = b.emit("tensor", lambda e, gB=gB, gC=gC, sh=sh: e.matmul(
                    pSC[:, 0:c.C],
                    bcb[0][:, gB * c.C + sh * 128:gB * c.C + (sh + 1) * 128],
                    bcb[0][:, gC * c.C:(gC + 1) * c.C]),
                    waits=bts + [scs_free, qprev] + ctoks[-1:], inc=TE)
                q2 = b.emit("scalar", lambda e, sh=sh: e.copy(
                    scsb[:, sh * c.C:(sh + 1) * c.C], pSC[:, 0:c.C]),
                    waits=[q1], inc=AC)
                qprev = q2
                sc_toks.append(q2)
            scs_free = sc_toks[-1]
            trts = []
            for sh in range(2):
                u1 = b.emit("tensor", lambda e, gB=gB, sh=sh: e.transpose(
                    ctx_m["pTRB"][:, 0:128],
                    bcb[0][:, gB * c.C + sh * 128:gB * c.C + (sh + 1) * 128],
                    ident_b[:]), waits=bts + bd_free + [ctx_m["ptrb_free"][0]],
                    inc=TE)
                u2 = b.emit("vector", lambda e, sh=sh: e.tensor_copy(
                    btokb[:, sh * 128:(sh + 1) * 128],
                    ctx_m["pTRB"][0:128, 0:128]), waits=[u1], inc=DV)
                ctx_m["ptrb_free"][0] = u2
                trts.append(u2)
            for hh in range(HG):
                h = g * HG + hh
                hp = h // HP2
                hoff = (h % HP2) * c.P
                iy = hp % 4
                ip = h % 2
                slot = hp % 2
                jc = h % 4
                cb = csb_tok[jc]
                if h + 3 < c.H:
                    csb_load(h + 3)
                e3 = b.emit("scalar", lambda e, ip=ip, jc=jc: e.activation(
                    odb[ip][:], csb[jc][:], AF.Exp),
                    waits=[cb, od_free[ip]], inc=AC)
                d1 = b.emit("vector", lambda e, h=h, ip=ip, jc=jc:
                            e.scalar_tensor_tensor(
                    etmp[ip][:, 0:c.C], csb[jc][:], ncstk[:, h:h + 1],
                    mng[:, 0:c.C], op0=ALU.add, op1=ALU.min),
                    waits=[cb, etmp_free[ip], mng_tok] + ctoks, inc=DV)
                d2 = b.emit("vector", lambda e, h=h, ip=ip, jc=jc:
                            e.scalar_tensor_tensor(
                    etmp[ip][:, c.C:2 * c.C], csb[jc][:],
                    ncstk[:, c.H + h:c.H + h + 1],
                    mng[:, c.C:2 * c.C], op0=ALU.add, op1=ALU.min),
                    waits=[cb], inc=DV)
                e1 = b.emit("scalar", lambda e, ip=ip: e.activation(
                    ebuf[ip][:, 0:c.C], etmp[ip][:, 0:c.C], AF.Exp),
                    waits=[d1, eb_free[ip]], inc=AC)
                e2 = b.emit("scalar", lambda e, ip=ip: e.activation(
                    ebuf[ip][:, c.C:2 * c.C], etmp[ip][:, c.C:2 * c.C], AF.Exp),
                    waits=[d2], inc=AC)
                csb_free[jc] = [d1, d2, e3]
                etmp_free[ip] = e2
                g1ts = []
                for sh in range(2):
                    g1t = b.emit("vector", lambda e, sh=sh, h=h, ip=ip:
                                 e.tensor_scalar(
                        gtb[ip][:, sh * c.C:(sh + 1) * c.C],
                        scsb[:, sh * c.C:(sh + 1) * c.C],
                        dttk[:, sh * c.H + h:sh * c.H + h + 1], None,
                        op0=ALU.mult),
                        waits=sc_toks + ctoks + [gt_free[ip]], inc=DV)
                    g1ts.append(g1t)
                v1 = b.emit("vector", lambda e, h=h, ip=ip: e.tensor_scalar(
                    bdb[:, ip * 256:ip * 256 + 128], btokb[:, 0:128],
                    ddmtk[:, h:h + 1], None, op0=ALU.mult),
                    waits=trts + ctoks + [bdb_rd[ip]], inc=DV)
                v2 = b.emit("vector", lambda e, h=h, ip=ip: e.tensor_scalar(
                    bdb[:, ip * 256 + 128:ip * 256 + 256], btokb[:, 128:256],
                    ddmtk[:, c.H + h:c.H + h + 1], None, op0=ALU.mult),
                    waits=trts + ctoks + [bdb_rd[ip]], inc=DV)
                c1 = b.emit("gpsimd", lambda e, gC=gC, ip=ip: e.tensor_mul(
                    cdb[ip][:], bcb[0][:, gC * c.C:(gC + 1) * c.C], odb[ip][:]),
                    waits=[e3, cd_free[ip]] + bts, inc=PL)
                od_free[ip] = c1
                g_toks = []
                for sh in range(2):
                    g2t = b.emit("vector", lambda e, sh=sh, ip=ip: e.tensor_mul(
                        gtb[ip][:, sh * c.C:(sh + 1) * c.C],
                        gtb[ip][:, sh * c.C:(sh + 1) * c.C],
                        ebuf[ip][:, sh * c.C:(sh + 1) * c.C]),
                        waits=[g1ts[sh], e1 if sh == 0 else e2], inc=DV)
                    g_toks.append(g2t)
                gt_free[ip] = g_toks[-1]
                eb_free[ip] = g_toks[1]
                m1 = b.emit("tensor", lambda e, h=h, ip=ip: e.matmul(
                    pS[ip][:, 0:c.P], bdb[:, ip * 256:ip * 256 + 128],
                    xtk[0][0][:, h * c.P:(h + 1) * c.P], start=True, stop=False),
                    waits=[v1] + xts + [pS_free[ip]], inc=None)
                m2 = b.emit("tensor", lambda e, h=h, ip=ip: e.matmul(
                    pS[ip][:, 0:c.P], bdb[:, ip * 256 + 128:ip * 256 + 256],
                    xtk[0][1][:, h * c.P:(h + 1) * c.P], start=False, stop=True),
                    waits=[v2], inc=TE)
                bd_free = [m2]
                bdb_rd[ip] = m2
                if hoff == 0:
                    xl = b.emit("scalar", lambda e, ch=ch, hp=hp, slot=slot: e.dma_start(
                        xjit[slot][:],
                        s_xh[hp * 128:(hp + 1) * 128, ch * c.C:(ch + 1) * c.C]),
                        waits=[xj_free[slot]], inc=XJ[slot], n=16)
                y1 = b.emit("tensor", lambda e, h=h, iy=iy, hoff=hoff, ip=ip: e.matmul(
                    pY[iy][hoff:hoff + c.P, 0:c.C],
                    xtk[0][0][:, h * c.P:(h + 1) * c.P], gtb[ip][:, 0:c.C],
                    start=True, stop=False),
                    waits=[g_toks[0]] + xts + [pY_free[iy]], inc=None)
                b.emit("tensor", lambda e, h=h, iy=iy, hoff=hoff, ip=ip: e.matmul(
                    pY[iy][hoff:hoff + c.P, 0:c.C],
                    xtk[0][1][:, h * c.P:(h + 1) * c.P], gtb[ip][:, c.C:2 * c.C],
                    start=False, stop=False), waits=[g_toks[1]], inc=None)
                b.emit("tensor", lambda e, h=h, iy=iy, hoff=hoff, ip=ip: e.matmul(
                    pY[iy][hoff:hoff + c.P, 0:c.C],
                    srun_bf[:, h * c.P:(h + 1) * c.P], cdb[ip][:],
                    start=False, stop=False), waits=[c1, cast_tok], inc=None)
                y4 = b.emit("tensor", lambda e, h=h, iy=iy, hoff=hoff, slot=slot:
                            e.matmul(
                    pY[iy][hoff:hoff + c.P, 0:c.C],
                    dball_all[hoff:hoff + 64, (h // 2) * c.P:(h // 2) * c.P + 64],
                    xjit[slot][hoff:hoff + c.P, :],
                    start=False, stop=True),
                    waits=[dba_tok, xl],
                    inc=(TE if hoff + c.P == 128 else None))
                xj_free[slot] = y4

                supd = b.emit("vector", lambda e, h=h, ch=ch, ip=ip:
                              e.scalar_tensor_tensor(
                    S_f32[:, h * c.P:(h + 1) * c.P],
                    S_f32[:, h * c.P:(h + 1) * c.P],
                    e_all[:, ch * c.H + h:ch * c.H + h + 1],
                    pS[ip][:, 0:c.P], op0=ALU.mult, op1=ALU.add),
                    waits=[m2, cast_tok, ea_tok], inc=DV)
                pS_free[ip] = supd
                st["supd_last"] = supd
                if hoff + c.P == 128:
                    flush_pair()

                    def mk_pair(iy=iy, slot=slot, ch=ch, hp=hp, ip=ip, y4=y4,
                                iy2=hp % 2):
                        def emitfn():
                            zl = b.emit("scalar", lambda e: e.dma_start(
                                zjit[slot][:],
                                s_z[hp * 128:(hp + 1) * 128,
                                    ch * c.C:(ch + 1) * c.C]),
                                waits=[zj_free[slot]], inc=ZJ[slot], n=16)
                            v1y = b.emit("vector", lambda e: e.tensor_mul(
                                ygb[iy2][:], pY[iy][:, 0:c.C], zjit[slot][:]),
                                waits=[y4, zl, yg_free[iy2]], inc=DV)
                            pY_free[iy] = v1y
                            zj_free[slot] = v1y
                            s1 = b.emit("scalar", lambda e: e.activation(
                                sqb[ip][:], ygb[iy2][:], AF.Square),
                                waits=[v1y, sqb_free[ip]], inc=AC)
                            if st["first_sq"]:
                                s2 = b.emit("gpsimd", lambda e: e.tensor_copy(
                                    rmsacc[:], sqb[ip][:]),
                                    waits=[s1, st["rms_mm_prev"]], inc=PL)
                                st["first_sq"] = False
                            else:
                                s2 = b.emit("gpsimd", lambda e: e.tensor_add(
                                    rmsacc[:], rmsacc[:], sqb[ip][:]),
                                    waits=[s1, st["sq_prev"]], inc=PL)
                            sqb_free[ip] = s2
                            st["sq_prev"] = s2
                            v2y = b.emit("scalar", lambda e: e.mul(
                                ygnb[iy2][:], ygb[iy2][:], nw_s[:, hp:hp + 1]),
                                waits=[v1y, ygn_free[iy2]], inc=AC)
                            yg_free[iy2] = v2y
                            ygn_free[iy2] = b.emit("gpsimd", lambda e: e.dma_start(
                                s_ygn[hp, :, ch * c.C:(ch + 1) * c.C],
                                ygnb[iy2][:]),
                                waits=[v2y], inc=YGST[iy2], n=16)
                        return emitfn
                    pending_pair[0] = mk_pair()
                    flush_pair()
                g1_goal = ((ch * c.H + h + 1) * len(g1_units)) // (c.NCH * c.H)
                g1_run(g1_goal - g1_pos[0])
        flush_pair()
        srun_readers = [st["supd_last"]] + bd_free
        r1 = b.emit("tensor", lambda e: e.matmul(
            pSC[0:1, 0:c.C], ones128_f[:], rmsacc[:]),
            waits=[st["sq_prev"], scs_free], inc=TE)
        r2a = b.emit("scalar", lambda e, ch=ch: e.activation(
            sqb[0][0:1, 0:c.C], pSC[0:1, 0:c.C], AF.Sqrt,
            bias=float(c.eps_rms), scale=1.0 / c.DI),
            waits=[r1, sqb_free[0], sqb_free[1]], inc=AC)
        r2v = b.emit("vector", lambda e, ch=ch: e.reciprocal(
            rs_piece[0:1, 0:c.C], sqb[0][0:1, 0:c.C]),
            waits=[r2a, rs_st_prev], inc=DV)
        r2 = b.emit("sync", lambda e, ch=ch: e.dma_start(
            s_rs[0:1, ch * c.C:(ch + 1) * c.C], rs_piece[0:1, 0:c.C]),
            waits=[r2v], inc=RSST, n=16)
        rs_st_prev = r2
        st["rms_mm_prev"] = r1
        scs_free = r2a
        sqb_free[0] = r2v
        for sh in range(2):
            xtk_free[sh] = bd_free[0]
        bcb_free[0] = bd_free[0]
    g1_run(len(g1_units))
    rs_tok = r2
    b.barrier()
    b.flush()
    esM.close()
    es_ssd.close()
    esG1.close()

    es_xn2 = contextlib.ExitStack()
    xn_fm = sb(es_xn2, "xn_fm2", [128, c.KD * c.T], BF16, side="right")
    XT2 = S("xntr2")
    for k in range(c.KD):
        b.emit("sync", lambda e, k=k: e.dma_start(
            xn_fm[:, k * c.T:(k + 1) * c.T],
            s_xn[:, k * 128:(k + 1) * 128], transpose=True), inc=XT2, n=16)
    b.barrier()
    b.flush()

    wgb = [sb(esG2, f"wgb{i}", [128, c.KD * 128], BF16) for i in range(2)]
    wub = [sb(esG2, f"wub{i}", [128, c.KD * 128], BF16) for i in range(2)]
    znb = [sb(esG2, f"znb{i}", [c.G_GU, 512], BF16) for i in range(2)]
    ZNL = [S("znl0"), S("znl1")]
    hbuf = sb(esG2, "hbuf", [128, c.T // 2], BF16)
    sgate = sb(esG2, "sgate", [128, c.T // 2], BF16)
    TH = c.T // 2
    pGU = [pt(esG2, f"pGU{i}", [128, TH]) for i in range(4)]
    WGL = [S("wgl0"), S("wgl1")]
    WUL = [S("wul0"), S("wul1")]
    HST = S("hst")
    wg_free = [None, None]
    wu_free = [None, None]
    pgu_free = [None, None, None, None]
    hb_free = None
    sg_free = None

    def wgu_src(side, mp):
        col = side * c.INTER + mp * 128
        return s_wgu[:, :, col:col + 128].transpose([1, 0, 2])

    zn_free = [None, None]
    for mp in range(c.KF):
        i = mp % 2
        lg = b.emit("sync", lambda e, i=i, mp=mp: e.dma_start(
            wgb[i][:].rearrange("p (k j) -> p k j", j=128), wgu_src(0, mp)),
            waits=[wg_free[i]], inc=WGL[i], n=16)
        lu = b.emit("sync", lambda e, i=i, mp=mp: e.dma_start(
            wub[i][:].rearrange("p (k j) -> p k j", j=128), wgu_src(1, mp)),
            waits=[wu_free[i]], inc=WUL[i], n=16)
        lz = None
        for qi, (hl, side) in enumerate(((0, 0), (0, 1), (1, 0), (1, 1))):
            lz = b.emit("sync", lambda e, i=i, mp=mp, qi=qi, hl=hl, side=side:
                        e.dma_start(
                znb[i][:, qi * 128:(qi + 1) * 128],
                s_zgu[hl, :, side * c.INTER + mp * 128:
                      side * c.INTER + (mp + 1) * 128]),
                waits=[zn_free[i], zn_tok] if qi == 0 else [], inc=ZNL[i], n=16)
        h_toks = []
        for th in range(2):
            pg, pu = pGU[2 * (th % 2)], pGU[2 * (th % 2) + 1]
            nbs = [(nb * 512, min(512, TH - nb * 512))
                   for nb in range((TH + 511) // 512)]
            mg = mu = None
            for k in range(c.KD):
                for nb, (no, nw) in enumerate(nbs):
                    co = th * TH + no
                    mg = b.emit("tensor", lambda e, i=i, k=k, co=co, no=no, nw=nw,
                                pg=pg: e.matmul(
                        pg[:, no:no + nw],
                        wgb[i][:, k * 128:(k + 1) * 128],
                        xn_fm[:, k * c.T + co:k * c.T + co + nw],
                        start=(k == 0), stop=False),
                        waits=([lg, pgu_free[2 * (th % 2)]] if k == 0 and nb == 0 else []),
                        inc=None)
                    mu = b.emit("tensor", lambda e, i=i, k=k, co=co, no=no, nw=nw,
                                pu=pu: e.matmul(
                        pu[:, no:no + nw],
                        wub[i][:, k * 128:(k + 1) * 128],
                        xn_fm[:, k * c.T + co:k * c.T + co + nw],
                        start=(k == 0), stop=False),
                        waits=([lu, pgu_free[2 * (th % 2) + 1]] if k == 0 and nb == 0 else []),
                        inc=None)
            for (xg_, zi, lastf) in ((xg_hi, 0, False), (xg_hi, 1, False),
                                     (xg_lo, 0, True)):
                for nb, (no, nw) in enumerate(nbs):
                    co = th * TH + no
                    mg = b.emit("tensor", lambda e, no=no, nw=nw, co=co, xg_=xg_,
                                zi=zi, i=i, pg=pg, lastf=lastf: e.matmul(
                        pg[:, no:no + nw],
                        znb[i][:, zi * 256:zi * 256 + 128], xg_[:, co:co + nw],
                        start=False, stop=lastf),
                        waits=[xg_tok, lz],
                        inc=(TE if lastf and nb == len(nbs) - 1 else None))
                    mu = b.emit("tensor", lambda e, no=no, nw=nw, co=co, xg_=xg_,
                                zi=zi, i=i, pu=pu, lastf=lastf: e.matmul(
                        pu[:, no:no + nw],
                        znb[i][:, zi * 256 + 128:zi * 256 + 256],
                        xg_[:, co:co + nw], start=False, stop=lastf),
                        waits=[],
                        inc=(TE if lastf and nb == len(nbs) - 1 else None))
            wg_free[i] = mg
            wu_free[i] = mu
            zn_free[i] = mu
            sg = b.emit("scalar", lambda e, pg=pg: e.activation(
                sgate[:], pg[:], AF.Silu), waits=[mg, sg_free], inc=AC)
            pgu_free[2 * (th % 2)] = sg
            hh2 = b.emit("vector", lambda e, pu=pu: e.tensor_mul(
                hbuf[:], sgate[:], pu[:]),
                waits=[sg, mu, hb_free], inc=DV)
            pgu_free[2 * (th % 2) + 1] = hh2
            sg_free = hh2
            hb_free = b.emit("sync", lambda e, mp=mp, th=th: e.dma_start(
                s_h[mp, :, th * TH:(th + 1) * TH], hbuf[:]),
                waits=[hh2], inc=HST, n=16)
            h_toks.append(hb_free)
    b.barrier()
    b.flush()
    esG2.close()
    es_xn2.close()
    es_mid.close()
    esO = contextlib.ExitStack()
    KBLK = 8
    nkb = (c.KDI + KBLK - 1) // KBLK
    ygn_blk = sb(esO, "ygn_blk", [128, KBLK * c.T], BF16)
    wo_blk = sb(esO, "wo_blk", [128, KBLK * c.D], BF16)
    evo = [sb(esO, f"evo{i}", [128, 2048], F32) for i in range(2)]
    pO = [pt(esO, f"pO{i}", [128, 2048]) for i in range(2)]
    YGL, WOL = S("ygl"), S("wol")
    UAC = [S("uac0"), S("uac1")]
    DH = (c.D + 2047) // 2048
    evo_free = [None, None]
    po_free = [None, None]
    blk_free = None
    for kb in range(nkb):
        nkk = min(KBLK, c.KDI - kb * KBLK)
        b.wait("gpsimd", (UAC[0], UAC[0].count))
        b.wait("gpsimd", (UAC[1], UAC[1].count))
        la = lb = None
        for kk in range(nkk):
            la = b.emit("sync", lambda e, kb=kb, kk=kk: e.dma_start(
                ygn_blk[:, kk * c.T:(kk + 1) * c.T], s_ygn[kb * KBLK + kk, :, :]),
                waits=[blk_free] if kk == 0 else [], inc=YGL, n=16)
            lb = b.emit("gpsimd", lambda e, kb=kb, kk=kk: e.dma_start(
                wo_blk[:, kk * c.D:(kk + 1) * c.D],
                woutT_d[(kb * KBLK + kk) * 128:(kb * KBLK + kk + 1) * 128, :]),
                waits=[blk_free] if kk == 0 else [], inc=WOL, n=16)
        lts = [la, lb]
        last_mm = None
        for tt in range(c.TT):
            for dh in range(DH):
                io = (tt * DH + dh) % 2
                dw = min(2048, c.D - dh * 2048)
                mm = None
                for kk in range(nkk):
                    for nb2 in range(dw // 512):
                        mm = b.emit("tensor", lambda e, kk=kk, tt=tt, dh=dh,
                                    nb2=nb2, io=io: e.matmul(
                            pO[io][:, nb2 * 512:(nb2 + 1) * 512],
                            ygn_blk[:, kk * c.T + tt * 128:kk * c.T + (tt + 1) * 128],
                            wo_blk[:, kk * c.D + dh * 2048 + nb2 * 512:
                                   kk * c.D + dh * 2048 + (nb2 + 1) * 512],
                            start=(kk == 0), stop=(kk == nkk - 1)),
                            waits=(lts + [po_free[io]]) if kk == 0 and nb2 == 0 else [],
                            inc=(TE if kk == nkk - 1 and nb2 == dw // 512 - 1 else None))
                ev = b.emit("scalar", lambda e, io=io, dw=dw: e.copy(
                    evo[io][:, 0:dw], pO[io][:, 0:dw]),
                    waits=[mm, evo_free[io]], inc=AC)
                po_free[io] = ev
                evo_free[io] = b.emit("gpsimd", lambda e, io=io, tt=tt, dh=dh,
                                      dw=dw, kb=kb: e.dma_start(
                    s_u[tt * 128:(tt + 1) * 128, dh * 2048:dh * 2048 + dw],
                    evo[io][:, 0:dw],
                    accum_op=(ALU.bypass if kb == 0 else ALU.add)),
                    waits=[ev], inc=UAC[io], n=16)
                last_mm = mm
        blk_free = last_mm
    b.barrier()
    b.flush()
    esO.close()

    esD = contextlib.ExitStack()
    FBLK = 8
    nfb = (c.KF + FBLK - 1) // FBLK
    qdb = sb(esD, "qdb", [128, c.D], U32)
    nibd = sb(esD, "nibd", [128, c.D], BF16)
    scdb = sb(esD, "scdb", [128, c.D], F32)
    scdbf = sb(esD, "scdbf", [128, c.D], BF16)
    ohd = sb(esD, "ohd", [128, c.G_D], BF16)
    gdf = sb(esD, "gdf", [128, c.KF], F32)
    xgd_hi = sb(esD, "xgd_hi", [c.G_D, c.T], BF16)
    xgd_lo = sb(esD, "xgd_lo", [c.G_D, c.T], BF16)
    xgd_f = sb(esD, "xgd_f", [c.G_D, c.T], F32)
    znd_hi = sb(esD, "znd_hi", [c.G_D, c.D], BF16)
    znd_lo = sb(esD, "znd_lo", [c.G_D, c.D], BF16)
    HL, QDL, SDL, ZDL = S("hl"), S("qdl"), S("sdl"), S("zdl")
    DAC = [S("dac0"), S("dac1")]
    esDz = contextlib.ExitStack()
    DZSL = min(2048, c.D)
    zqd_s = sb(esDz, "zqds", [c.G_D, c.D // 8], U32)
    zfd = sb(esDz, "zfd", [c.G_D, DZSL], F32)
    ztud = sb(esDz, "ztud", [c.G_D, DZSL // 8], U32)
    ztfd = sb(esDz, "ztfd", [c.G_D, DZSL // 8], F32)
    sczd = sb(esDz, "sczd", [c.G_D, DZSL], F32)
    ZQDL = S("zqdl")
    zqt = b.emit("sync", lambda e: e.dma_start(zqd_s[:], qz_d_d[:]), inc=ZQDL, n=16)
    znd_tok = None
    prevs = []
    o = 0
    while o < c.D:
        w = min(DZSL, c.D - o)
        lsc = b.emit("sync", lambda e, o=o, w=w: e.dma_start(
            sczd[:, 0:w], sc_d_d[:, o:o + w]), waits=prevs, inc=ZDL, n=16)
        ztoks = []
        for sft in range(8):
            u1 = b.emit("vector", lambda e, o=o, w=w, sft=sft: e.tensor_scalar(
                ztud[:, 0:w // 8], zqd_s[:, o // 8:(o + w) // 8],
                (15 << (4 * sft)), None, op0=ALU.bitwise_and),
                waits=[zqt] + ztoks[-3:] + prevs, inc=DV)
            u2 = b.emit("vector", lambda e, w=w: e.tensor_copy(
                ztfd[:, 0:w // 8], ztud[:, 0:w // 8]), waits=[u1], inc=DV)
            u3 = b.emit("vector", lambda e, w=w, sft=sft: e.tensor_scalar_mul(
                zfd[:, sft:w:8], ztfd[:, 0:w // 8],
                float(2.0 ** (-4 * sft))), waits=[u2], inc=DV)
            ztoks += [u1, u2, u3]
        t = b.emit("vector", lambda e, w=w: e.tensor_scalar_add(
            zfd[:, 0:w], zfd[:, 0:w], 1.0), waits=ztoks, inc=DV)
        t = b.emit("vector", lambda e, w=w: e.tensor_mul(
            zfd[:, 0:w], zfd[:, 0:w], sczd[:, 0:w]), waits=[t, lsc], inc=DV)
        t = b.emit("vector", lambda e, w=w: e.tensor_scalar_mul(
            zfd[:, 0:w], zfd[:, 0:w], -1.0), waits=[t], inc=DV)
        th = b.emit("vector", lambda e, o=o, w=w: e.tensor_copy(
            znd_hi[:, o:o + w], zfd[:, 0:w]), waits=[t], inc=DV)
        t2 = b.emit("vector", lambda e, o=o, w=w: e.tensor_sub(
            sczd[:, 0:w], zfd[:, 0:w], znd_hi[:, o:o + w]), waits=[th], inc=DV)
        tl = b.emit("vector", lambda e, o=o, w=w: e.tensor_copy(
            znd_lo[:, o:o + w], sczd[:, 0:w]), waits=[t2], inc=DV)
        znd_tok = tl
        prevs = [t2, tl]
        o += w
    b.barrier()
    b.flush()
    esDz.close()
    h_blk = sb(esD, "h_blk", [128, FBLK * c.T], BF16)
    wd_blk = sb(esD, "wd_blk", [128, FBLK * c.D], BF16)
    evd = [sb(esD, f"evd{i}", [128, 2048], F32) for i in range(2)]
    pD = [pt(esD, f"pD{i}", [128, 2048]) for i in range(2)]
    tg = b.emit("vector", lambda e: e.tensor_copy(gdf[:], gidx_d_s[:]), inc=DV)
    ev_free = [None, None]
    pd_free = [None, None]
    hblk_free = None
    wd_free = None
    nibd_free = None
    scd_free = None
    oh_free = None
    for fb in range(nfb):
        nkk = min(FBLK, c.KF - fb * FBLK)
        b.wait("gpsimd", (DAC[0], DAC[0].count))
        b.wait("gpsimd", (DAC[1], DAC[1].count))
        lh_last = None
        dq = []
        for kk in range(nkk):
            kf = fb * FBLK + kk
            lh_last = b.emit("sync", lambda e, kf=kf, kk=kk: e.dma_start(
                h_blk[:, kk * c.T:(kk + 1) * c.T], s_h[kf, :, :]),
                waits=[hblk_free] if kk == 0 else [], inc=HL, n=16)
            q1 = b.emit("gpsimd", lambda e, kf=kf: e.dma_start(
                qdb[:], qw_d_d[kf, :, :]), waits=[nibd_free], inc=QDL, n=16)
            q2 = b.emit("gpsimd", lambda e, kf=kf: e.indirect_dma_start(
                scdb[:], None, sc_d_d[:, :], IOA(gidx_d_s[:, kf:kf + 1], 0)),
                waits=[scd_free], inc=SDL, n=16)
            d1 = b.emit("vector", lambda e: e.tensor_scalar(
                qdb[:], qdb[:], m15[:], None, op0=ALU.bitwise_and),
                waits=[q1], inc=DV)
            d2 = b.emit("vector", lambda e: e.tensor_copy(nibd[:], qdb[:]),
                        waits=[d1, nibd_free], inc=DV)
            nibd_free = d2
            d3 = b.emit("vector", lambda e: e.tensor_scalar_mul(
                nibd[:], nibd[:], minv[:]), waits=[d2], inc=DV)
            d4 = b.emit("vector", lambda e: e.tensor_copy(scdbf[:], scdb[:]),
                        waits=[q2, scd_free], inc=DV)
            scd_free = d4
            d5 = b.emit("vector", lambda e, kk=kk: e.tensor_mul(
                wd_blk[:, kk * c.D:(kk + 1) * c.D], nibd[:], scdbf[:]),
                waits=([d3, d4, wd_free] if kk == 0 else [d3, d4]), inc=DV)
            nibd_free = d5
            scd_free = d5
            dq.append(d5)
        mmx = None
        for kk in range(nkk):
            kf = fb * FBLK + kk
            otk = b.emit("vector", lambda e, kf=kf: e.tensor_scalar(
                ohd[:], iota_d[:], gdf[:, kf:kf + 1], None, op0=ALU.is_equal),
                waits=[tg, oh_free], inc=DV)
            for nb2 in range(c.T // 512):
                mmx = b.emit("tensor", lambda e, kk=kk, nb2=nb2: e.matmul(
                    pD[0][0:c.G_D, nb2 * 512:(nb2 + 1) * 512], ohd[:],
                    h_blk[:, kk * c.T + nb2 * 512:kk * c.T + (nb2 + 1) * 512],
                    start=(kk == 0), stop=(kk == nkk - 1)),
                    waits=([otk, lh_last, pd_free[0]]) if kk == 0 and nb2 == 0 else [otk],
                    inc=TE)
            oh_free = mmx
        xt1 = b.emit("vector", lambda e: e.tensor_copy(
            xgd_f[:], pD[0][0:c.G_D, 0:c.T]), waits=[mmx], inc=DV)
        xt2 = b.emit("vector", lambda e: e.tensor_copy(xgd_hi[:], xgd_f[:]),
                     waits=[xt1], inc=DV)
        xt3 = b.emit("vector", lambda e: e.tensor_sub(xgd_f[:], xgd_f[:], xgd_hi[:]),
                     waits=[xt2], inc=DV)
        xgd_tok = b.emit("vector", lambda e: e.tensor_copy(xgd_lo[:], xgd_f[:]),
                         waits=[xt3], inc=DV)
        pd_free[0] = xgd_tok
        last_mm = None
        for tt in range(c.TT):
            for dh in range(DH):
                io = (tt * DH + dh) % 2
                dw = min(2048, c.D - dh * 2048)
                mm = None
                for kk in range(nkk):
                    for nb2 in range(dw // 512):
                        mm = b.emit("tensor", lambda e, kk=kk, tt=tt, dh=dh,
                                    nb2=nb2, io=io: e.matmul(
                            pD[io][:, nb2 * 512:(nb2 + 1) * 512],
                            h_blk[:, kk * c.T + tt * 128:kk * c.T + (tt + 1) * 128],
                            wd_blk[:, kk * c.D + dh * 2048 + nb2 * 512:
                                   kk * c.D + dh * 2048 + (nb2 + 1) * 512],
                            start=(kk == 0), stop=False),
                            waits=(dq + [pd_free[io], xgd_tok, lh_last]) if kk == 0 and nb2 == 0 else [],
                            inc=None)
                for ci2, (xg_, zn_, lastf) in enumerate(
                        ((xgd_hi, znd_hi, False), (xgd_hi, znd_lo, False),
                         (xgd_lo, znd_hi, True))):
                    for nb2 in range(dw // 512):
                        mm = b.emit("tensor", lambda e, xg_=xg_, zn_=zn_, tt=tt,
                                    dh=dh, nb2=nb2, io=io, lastf=lastf: e.matmul(
                            pD[io][:, nb2 * 512:(nb2 + 1) * 512],
                            xg_[:, tt * 128:(tt + 1) * 128],
                            zn_[:, dh * 2048 + nb2 * 512:dh * 2048 + (nb2 + 1) * 512],
                            start=False, stop=lastf), waits=[znd_tok],
                            inc=(TE if lastf and nb2 == dw // 512 - 1 else None))
                ev = b.emit("scalar", lambda e, io=io, dw=dw: e.copy(
                    evd[io][:, 0:dw], pD[io][:, 0:dw]),
                    waits=[mm, ev_free[io]], inc=AC)
                pd_free[io] = ev
                ev_free[io] = b.emit("gpsimd", lambda e, io=io, tt=tt, dh=dh,
                                     dw=dw, fb=fb: e.dma_start(
                    s_dn[tt * 128:(tt + 1) * 128, dh * 2048:dh * 2048 + dw],
                    evd[io][:, 0:dw],
                    accum_op=(ALU.bypass if fb == 0 else ALU.add)),
                    waits=[ev], inc=DAC[io], n=16)
                last_mm = mm
        hblk_free = last_mm
        wd_free = last_mm
    b.barrier()
    b.flush()
    esD.close()

    esC = contextlib.ExitStack()
    xc2 = [sb(esC, f"xc2{i}", [128, c.D], F32) for i in range(2)]
    xnc = [sb(esC, f"xnc{i}", [128, c.D], BF16) for i in range(2)]
    uc = [sb(esC, f"uc{i}", [128, c.D], F32) for i in range(2)]
    dnc = [sb(esC, f"dnc{i}", [128, c.D], F32) for i in range(2)]
    gsil = sb(esC, "gsil", [128, c.D], F32)
    sbb = sb(esC, "sbb", [128, c.D], F32)
    ob = [sb(esC, f"ob{i}", [128, c.D], F32) for i in range(2)]
    rs_col = sb(esC, "rs_col", [128, c.TT], F32)
    rs_sb = sb(esC, "rs_sb", [1, 128], F32)
    RSL = S("rsl")
    pC2 = pt(esC, "pC2", [128, 512])
    CLX = [S("clx0"), S("clx1")]
    CLN = [S("cln0"), S("cln1")]
    CLU = [S("clu0"), S("clu1")]
    CLD = [S("cld0"), S("cld1")]
    OST = [S("ost0"), S("ost1")]
    rstoks = []
    rprev = None
    rld_prev = None
    for tt in range(c.TT):
        rld = b.emit("sync", lambda e, tt=tt: e.dma_start(
            rs_sb[0:1, 0:128], s_rs[0:1, tt * 128:(tt + 1) * 128]),
            waits=[rs_tok, rld_prev], inc=RSL, n=16)
        rt = b.emit("tensor", lambda e, tt=tt: e.transpose(
            pC2[0:128, 0:1], rs_sb[0:1, 0:128], ident_f[0:1, 0:1]),
            waits=[rprev, rld], inc=TE)
        rld_prev = rt
        rprev = b.emit("vector", lambda e, tt=tt: e.tensor_copy(
            rs_col[:, tt:tt + 1], pC2[0:128, 0:1]), waits=[rt], inc=DV)
        rstoks.append(rprev)
    rsc_tok = rstoks[-1]
    x_free = [None, None]
    xn_free = [None, None]
    u_free = [None, None]
    dn_free = [None, None]
    ob_free = [None, None]
    gs_free = None
    sb_free = None
    for tt in range(c.TT):
        i = tt % 2
        l1 = b.emit("sync", lambda e, i=i, tt=tt: e.dma_start(
            xc2[i][:], x_d[tt * 128:(tt + 1) * 128, :]), waits=[x_free[i]],
            inc=CLX[i], n=16)
        l2 = b.emit("sync", lambda e, i=i, tt=tt: e.dma_start(
            xnc[i][:], s_xn[tt * 128:(tt + 1) * 128, :]), waits=[xn_free[i]],
            inc=CLN[i], n=16)
        l3 = b.emit("sync", lambda e, i=i, tt=tt: e.dma_start(
            uc[i][:], s_u[tt * 128:(tt + 1) * 128, :]), waits=[u_free[i]],
            inc=CLU[i], n=16)
        l4 = b.emit("sync", lambda e, i=i, tt=tt: e.dma_start(
            dnc[i][:], s_dn[tt * 128:(tt + 1) * 128, :]), waits=[dn_free[i]],
            inc=CLD[i], n=16)
        g1 = b.emit("scalar", lambda e, i=i: e.activation(
            gsil[:], dnc[i][:], AF.Silu), waits=[l4, gs_free], inc=AC)
        dn_free[i] = g1
        s1 = b.emit("vector", lambda e, i=i, tt=tt: e.scalar_tensor_tensor(
            sbb[:], uc[i][:], rs_col[:, tt:tt + 1], xnc[i][:],
            op0=ALU.mult, op1=ALU.add), waits=[l3, l2, rsc_tok, sb_free], inc=DV)
        u_free[i] = s1
        xn_free[i] = s1
        s2 = b.emit("vector", lambda e: e.tensor_mul(sbb[:], sbb[:], gsil[:]),
                    waits=[s1, g1], inc=DV)
        gs_free = s2
        o1 = b.emit("vector", lambda e, i=i: e.tensor_add(
            ob[i][:], sbb[:], xc2[i][:]), waits=[s2, l1, ob_free[i]], inc=DV)
        sb_free = o1
        x_free[i] = o1
        ob_free[i] = b.emit("sync", lambda e, i=i, tt=tt: e.dma_start(
            out_d[tt * 128:(tt + 1) * 128, :], ob[i][:]), waits=[o1], inc=OST[i], n=16)
    b.wait("gpsimd", (OST[0], OST[0].count))
    b.wait("gpsimd", (OST[1], OST[1].count))
    b.flush()
    esC.close()
    return nc


def kernel(**inputs):
    cfg = CFG_REAL
    in_maps = make_in_maps(cfg, inputs)
    nc = build(cfg)
    res = run_bass_kernel_spmd(nc, in_maps, core_ids=list(range(cfg.n_cores)))
    x = np.asarray(inputs["L0_attn_hidden_states"])
    out = np.empty((cfg.batch, 2 * cfg.T, cfg.D), dtype=np.float32)
    for core in range(cfg.n_cores):
        bb, half = core // 2, core % 2
        out[bb, half * cfg.T:(half + 1) * cfg.T, :] = res.results[core]["out"]
    return out.reshape(x.shape)
